# version 04
# baseline (speedup 1.0000x reference)
import numpy as np
import ml_dtypes
from contextlib import ExitStack

import concourse.bass as bass
import concourse.mybir as mybir
from concourse.bass_utils import run_bass_kernel_spmd
from concourse.alu_op_type import AluOpType

F32 = mybir.dt.float32
BF16 = mybir.dt.bfloat16
I8 = mybir.dt.int8
I32 = mybir.dt.int32
AF = mybir.ActivationFunctionType
AX = mybir.AxisListType
ET = mybir.EngineType
MUL = AluOpType.mult
ADD = AluOpType.add
SUB = AluOpType.subtract

D = 2048
S = 4096
NCH = 16
QL = 1024
HD = 64
INTER = 8192
EPS = 1e-5
EXP_SHIFT = -4.0
DEBUG = False


def build_nc():
    nc = bass.Bass("TRN2", target_bir_lowering=False)

    xT = nc.dram_tensor("xT", [D, S], F32, kind="ExternalInput").ap()
    xq = nc.dram_tensor("xq", [D, QL], F32, kind="ExternalInput").ap()
    wqTs = nc.dram_tensor("wqTs", [4, D, 512], F32, kind="ExternalInput").ap()
    wkT = nc.dram_tensor("wkT", [D, 512], F32, kind="ExternalInput").ap()
    wvT = nc.dram_tensor("wvT", [D, 512], F32, kind="ExternalInput").ap()
    woSW = nc.dram_tensor("woSW", [16, 128, 16, 128], F32, kind="ExternalInput").ap()
    wgSW = nc.dram_tensor("wgSW", [64, 128, 16, 128], I8, kind="ExternalInput").ap()
    wuSW = nc.dram_tensor("wuSW", [64, 128, 16, 128], I8, kind="ExternalInput").ap()
    wdSW = nc.dram_tensor("wdSW", [16, 128, 64, 128], I8, kind="ExternalInput").ap()
    sgT = nc.dram_tensor("sgT", [128, 64], F32, kind="ExternalInput").ap()
    supT = nc.dram_tensor("supT", [128, 64], F32, kind="ExternalInput").ap()
    sdT = nc.dram_tensor("sdT", [128, 16], F32, kind="ExternalInput").ap()
    ck8 = nc.dram_tensor("ck8", [S, 256], F32, kind="ExternalInput").ap()
    sk8 = nc.dram_tensor("sk8", [S, 256], F32, kind="ExternalInput").ap()
    cq8 = nc.dram_tensor("cq8", [QL, 256], F32, kind="ExternalInput").ap()
    sq8 = nc.dram_tensor("sq8", [QL, 256], F32, kind="ExternalInput").ap()
    maskI = nc.dram_tensor("maskI", [4, 128, 512], F32, kind="ExternalInput").ap()
    ident = nc.dram_tensor("ident", [128, 128], F32, kind="ExternalInput").ap()
    out_fm = nc.dram_tensor("out_fm", [D, QL], F32, kind="ExternalOutput").ap()
    if DEBUG:
        dQ = nc.dram_tensor("dQ", [128, 16 * 1024], F32, kind="ExternalOutput").ap()
        dK = nc.dram_tensor("dK", [128, 4 * 4096], F32, kind="ExternalOutput").ap()
        dV = nc.dram_tensor("dV", [128, 32 * 512], F32, kind="ExternalOutput").ap()
        dO = nc.dram_tensor("dO", [128, 16 * 1024], F32, kind="ExternalOutput").ap()
    y1sp = nc.dram_tensor("y1sp", [D, QL], F32, kind="Internal").ap()

    ctx = ExitStack()
    nc._kb_ctx = ctx
    SEM = {}
    CNT = {}
    for s_ in ("LG", "LS", "T", "SC", "V", "STO"):
        SEM[s_] = ctx.enter_context(nc.semaphore(s_))
        CNT[s_] = 0

    def INC(instr, s_, k=1):
        instr.then_inc(SEM[s_], k)
        CNT[s_] += k
        return CNT[s_]

    E = ctx.enter_context
    id_bf = E(nc.sbuf_tensor("id_bf", [128, 128], BF16))
    ones64 = E(nc.sbuf_tensor("ones64", [128, 64], BF16))
    ones1b = E(nc.sbuf_tensor("ones1b", [128, 1], BF16))
    ones1f = E(nc.sbuf_tensor("ones1f", [1, 128], F32))
    sg_sb = E(nc.sbuf_tensor("sg_sb", [128, 64], F32))
    sup_sb = E(nc.sbuf_tensor("sup_sb", [128, 64], F32))
    sd_sb = E(nc.sbuf_tensor("sd_sb", [128, 16], F32))
    consts = E(nc.sbuf_tensor("consts", [128, 2], F32))
    scrB = E(nc.sbuf_tensor("scrB", [128, 1024], F32))
    O_fm = E(nc.sbuf_tensor("O_fm", [128, 16 * 1024], BF16))

    INC(nc.sync.dma_start(out=sg_sb[:], in_=sgT), "LS", 16)
    INC(nc.sync.dma_start(out=sup_sb[:], in_=supT), "LS", 16)
    INC(nc.sync.dma_start(out=sd_sb[:], in_=sdT), "LS", 16)
    INC(nc.gpsimd.dma_start(out=id_bf[:], in_=ident), "LG", 16)
    nc.vector.memset(consts[:, 0:1], EXP_SHIFT)
    nc.vector.memset(consts[:, 1:2], EPS)
    nc.vector.memset(ones64[:], 1.0)
    nc.vector.memset(ones1b[:], 1.0)
    INC(nc.vector.memset(ones1f[:], 1.0), "V", 1)

    jval = nc.partition_id([ET.PE, ET.Activation, ET.DVE]) % 4

    def rope6(psrc, dst, ct, st, scrA):
        g3 = lambda ap: ap.rearrange("p (g c two) -> p g c two", g=8, two=2)
        r_ = g3(psrc)[:, :, :, 0]
        i_ = g3(psrc)[:, :, :, 1]
        c_ = ct.rearrange("p (g c) -> p g c", g=8)
        s_ = st.rearrange("p (g c) -> p g c", g=8)
        t1 = scrA.ap()[:, 0:256].rearrange("p (g c) -> p g c", g=8)
        t2 = scrA.ap()[:, 256:512].rearrange("p (g c) -> p g c", g=8)
        t3 = scrA.ap()[:, 512:768].rearrange("p (g c) -> p g c", g=8)
        t4 = scrA.ap()[:, 768:1024].rearrange("p (g c) -> p g c", g=8)
        nc.vector.tensor_tensor(t1, r_, c_, MUL)
        nc.vector.tensor_tensor(t2, i_, s_, MUL)
        nc.vector.tensor_tensor(t3, r_, s_, MUL)
        nc.vector.tensor_tensor(t4, i_, c_, MUL)
        od = g3(dst)
        nc.vector.tensor_tensor(od[:, :, :, 0], t1, t2, SUB)
        return nc.vector.tensor_tensor(od[:, :, :, 1], t3, t4, ADD)

    with ExitStack() as s1:
        E1 = s1.enter_context
        scrA = E1(nc.sbuf_tensor("scrA", [128, 1024], F32))
        Q_all = E1(nc.sbuf_tensor("Q_all", [128, 16 * 1024], BF16))
        K_all = E1(nc.sbuf_tensor("K_all", [128, 4 * 4096], BF16))
        V_tm = E1(nc.sbuf_tensor("V_tm", [128, 32 * 512], BF16))
        tabc = [E1(nc.sbuf_tensor(f"tabc{i}", [128, 256], F32)) for i in range(2)]
        tabs = [E1(nc.sbuf_tensor(f"tabs{i}", [128, 256], F32)) for i in range(2)]
        stag = [E1(nc.sbuf_tensor(f"stag{i}", [128, 512], BF16)) for i in range(2)]

        with ExitStack() as s3, \
             nc.psum_tensor("psQ0", [128, 512], F32) as psQ0, \
             nc.psum_tensor("psQ1", [128, 512], F32) as psQ1, \
             nc.psum_tensor("psT", [128, 1024], BF16) as psT:
            E3 = s3.enter_context
            xq_bf = E3(nc.sbuf_tensor("xq_bf", [128, 16 * 1024], BF16))
            wq_sb = E3(nc.sbuf_tensor("wq_sb", [128, 16 * 512], BF16))
            psQ = [psQ0, psQ1]

            xq3 = xq.rearrange("(c p) t -> p c t", p=128)
            xqb3 = xq_bf.ap().rearrange("p (c t) -> p c t", t=1024)
            for hh in range(4):
                INC(nc.gpsimd.dma_start(out=xqb3[:, 4 * hh:4 * hh + 4, :],
                                        in_=xq3[:, 4 * hh:4 * hh + 4, :]), "LG", 16)
            xq_ready = CNT["LG"]

            rope_done = {}
            tpose_done = {}
            evac_done = {}
            mm_done = {}
            tab_done = {}
            wq_done = {}
            wq3 = wq_sb.ap().rearrange("p (c n) -> p c n", n=512)

            for s_ in range(4):
                if s_ > 0:
                    nc.gpsimd.wait_ge(SEM["T"], mm_done[s_ * 8 - 1])
                src = wqTs[s_].rearrange("(c p) n -> p c n", p=128)
                for hh in range(2):
                    INC(nc.gpsimd.dma_start(out=wq3[:, 8 * hh:8 * hh + 8, :],
                                            in_=src[:, 8 * hh:8 * hh + 8, :]), "LG", 16)
                wq_done[s_] = CNT["LG"]
                for tc in range(8):
                    u = s_ * 8 + tc
                    par = u % 2
                    if u >= 2:
                        nc.sync.wait_ge(SEM["V"], rope_done[u - 2])
                    INC(nc.sync.dma_start(out=tabc[par][:], in_=cq8[tc * 128:(tc + 1) * 128, :]), "LS", 16)
                    INC(nc.sync.dma_start(out=tabs[par][:], in_=sq8[tc * 128:(tc + 1) * 128, :]), "LS", 16)
                    tab_done[u] = CNT["LS"]

                    nc.tensor.wait_ge(SEM["LG"], max(xq_ready, wq_done[s_]))
                    if u >= 2:
                        nc.tensor.wait_ge(SEM["V"], rope_done[u - 2])
                    for c in range(NCH):
                        mm = nc.tensor.matmul(psQ[par][:], xq_bf[:, c * 1024 + tc * 128: c * 1024 + (tc + 1) * 128],
                                              wq_sb[:, c * 512:(c + 1) * 512],
                                              start=(c == 0), stop=(c == NCH - 1))
                    mm_done[u] = INC(mm, "T")

                    nc.vector.wait_ge(SEM["T"], mm_done[u])
                    nc.vector.wait_ge(SEM["LS"], tab_done[u])
                    if u >= 2:
                        nc.vector.wait_ge(SEM["T"], tpose_done[u - 2])
                    rope_done[u] = INC(rope6(psQ[par][:], stag[par][:], tabc[par][:], tabs[par][:], scrA), "V")

                    nc.tensor.wait_ge(SEM["V"], rope_done[u])
                    if u >= 2:
                        nc.tensor.wait_ge(SEM["V"], evac_done[u - 2])
                    off = par * 512
                    for p in range(4):
                        tp = nc.tensor.transpose(psT[:, off + p * 128: off + (p + 1) * 128],
                                                 stag[par][:, p * 128:(p + 1) * 128], id_bf[:])
                    tpose_done[u] = INC(tp, "T")

                    nc.vector.wait_ge(SEM["T"], tpose_done[u])
                    dst = Q_all.ap().rearrange("p (g t) -> p g t", t=1024)[:, 4 * s_:4 * s_ + 4, tc * 128:(tc + 1) * 128]
                    src_ = psT.ap()[:, off:off + 512].rearrange("p (g t) -> p g t", t=128)
                    evac_done[u] = INC(nc.vector.tensor_copy(dst, src_), "V")
            p1b_end_V = CNT["V"]

        with ExitStack() as s2, \
             nc.psum_tensor("psK0", [128, 512], F32) as psK0, \
             nc.psum_tensor("psK1", [128, 512], F32) as psK1, \
             nc.psum_tensor("psV0", [128, 512], F32) as psV0, \
             nc.psum_tensor("psV1", [128, 512], F32) as psV1, \
             nc.psum_tensor("psTa", [128, 1024], BF16) as psTa:
            E2 = s2.enter_context
            xblk = [E2(nc.sbuf_tensor(f"xblk{i}", [128, 16 * 256], BF16)) for i in range(2)]
            wk_sb = E2(nc.sbuf_tensor("wk_sb", [128, 16 * 512], BF16))
            wv_sb = E2(nc.sbuf_tensor("wv_sb", [128, 16 * 512], BF16))
            psK = [psK0, psK1]
            psV = [psV0, psV1]

            for (w_sb, w_src) in ((wk_sb, wkT), (wv_sb, wvT)):
                src = w_src.rearrange("(c p) n -> p c n", p=128)
                d3 = w_sb.ap().rearrange("p (c n) -> p c n", n=512)
                nc.gpsimd.wait_ge(SEM["V"], p1b_end_V)
                for hh in range(2):
                    INC(nc.gpsimd.dma_start(out=d3[:, 8 * hh:8 * hh + 8, :],
                                            in_=src[:, 8 * hh:8 * hh + 8, :]), "LG", 16)
            wv_ready = CNT["LG"]

            xT3 = xT.rearrange("(c p) t -> p c t", p=128)
            xb_done = {}
            kmm_done = {}
            vmm_done = {}
            ropek_done = {}
            vevac_done = {}
            ktp_done = {}
            kevac_done = {}
            tabk_done = {}

            def load_xblk(blk):
                bpar = blk % 2
                if blk >= 2:
                    nc.gpsimd.wait_ge(SEM["T"], vmm_done[(blk - 2) * 2 + 1])
                d3 = xblk[bpar].ap().rearrange("p (c t) -> p c t", t=256)
                for hh in range(2):
                    INC(nc.gpsimd.dma_start(out=d3[:, 8 * hh:8 * hh + 8, :],
                                            in_=xT3[:, 8 * hh:8 * hh + 8, blk * 256:(blk + 1) * 256]), "LG", 16)
                xb_done[blk] = CNT["LG"]

            load_xblk(0)
            load_xblk(1)
            for tc in range(32):
                blk = tc // 2
                bpar = blk % 2
                par = tc % 2
                toff = (tc % 2) * 128

                if tc < 2:
                    nc.sync.wait_ge(SEM["V"], p1b_end_V)
                if tc >= 2:
                    nc.sync.wait_ge(SEM["V"], ropek_done[tc - 2])
                INC(nc.sync.dma_start(out=tabc[par][:], in_=ck8[tc * 128:(tc + 1) * 128, :]), "LS", 16)
                INC(nc.sync.dma_start(out=tabs[par][:], in_=sk8[tc * 128:(tc + 1) * 128, :]), "LS", 16)
                tabk_done[tc] = CNT["LS"]

                nc.tensor.wait_ge(SEM["LG"], max(wv_ready, xb_done[blk]))
                if tc < 2:
                    nc.tensor.wait_ge(SEM["V"], p1b_end_V)
                if tc >= 2:
                    nc.tensor.wait_ge(SEM["V"], ropek_done[tc - 2])
                    nc.tensor.wait_ge(SEM["V"], vevac_done[tc - 2])
                for c in range(NCH):
                    mm = nc.tensor.matmul(psK[par][:], xblk[bpar][:, c * 256 + toff: c * 256 + toff + 128],
                                          wk_sb[:, c * 512:(c + 1) * 512],
                                          start=(c == 0), stop=(c == NCH - 1))
                kmm_done[tc] = INC(mm, "T")
                for c in range(NCH):
                    mm = nc.tensor.matmul(psV[par][:], xblk[bpar][:, c * 256 + toff: c * 256 + toff + 128],
                                          wv_sb[:, c * 512:(c + 1) * 512],
                                          start=(c == 0), stop=(c == NCH - 1))
                vmm_done[tc] = INC(mm, "T")

                nc.vector.wait_ge(SEM["T"], kmm_done[tc])
                nc.vector.wait_ge(SEM["LS"], tabk_done[tc])
                if tc >= 2:
                    nc.vector.wait_ge(SEM["T"], ktp_done[tc - 2])
                ropek_done[tc] = INC(rope6(psK[par][:], stag[par][:], tabc[par][:], tabs[par][:], scrA), "V")

                nc.vector.wait_ge(SEM["T"], vmm_done[tc])
                vevac_done[tc] = INC(nc.vector.tensor_copy(V_tm[:, tc * 512:(tc + 1) * 512], psV[par][:]), "V")

                nc.tensor.wait_ge(SEM["V"], ropek_done[tc])
                if tc >= 2:
                    nc.tensor.wait_ge(SEM["V"], kevac_done[tc - 2])
                off = par * 512
                for p in range(4):
                    tp = nc.tensor.transpose(psTa[:, off + p * 128: off + (p + 1) * 128],
                                             stag[par][:, p * 128:(p + 1) * 128], id_bf[:])
                ktp_done[tc] = INC(tp, "T")
                nc.vector.wait_ge(SEM["T"], ktp_done[tc])
                dst = K_all.ap().rearrange("p (b t) -> p b t", t=4096)[:, :, tc * 128:(tc + 1) * 128]
                src_ = psTa.ap()[:, off:off + 512].rearrange("p (b t) -> p b t", t=128)
                kevac_done[tc] = INC(nc.vector.tensor_copy(dst, src_), "V")
                if tc % 2 == 1 and blk + 2 < 16:
                    load_xblk(blk + 2)
            p1a_end = dict(CNT)
            if DEBUG:
                nc.gpsimd.wait_ge(SEM["V"], p1a_end["V"])
                INC(nc.gpsimd.dma_start(out=dQ, in_=Q_all[:]), "STO", 16)
                INC(nc.gpsimd.dma_start(out=dK, in_=K_all[:]), "STO", 16)
                INC(nc.gpsimd.dma_start(out=dV, in_=V_tm[:]), "STO", 16)

        with ExitStack() as s4, \
             nc.psum_tensor("psS0", [128, 1024], F32) as psS0, \
             nc.psum_tensor("psS1", [128, 1024], F32) as psS1, \
             nc.psum_tensor("psO", [128, 1024], F32) as psO, \
             nc.psum_tensor("psL", [128, 1024], F32) as psL:
            E4 = s4.enter_context
            PT = [E4(nc.sbuf_tensor(f"PT{i}", [128, 1024], BF16)) for i in range(2)]
            PTt = [E4(nc.sbuf_tensor(f"PTt{i}", [128, 512], BF16)) for i in range(2)]
            linv = E4(nc.sbuf_tensor("linv", [128, 1024], F32))
            lcp = E4(nc.sbuf_tensor("lcp", [128, 1024], F32))
            mask_bf = E4(nc.sbuf_tensor("mask_bf", [128, 4 * 512], BF16))
            psS = [psS0, psS1]

            nc.gpsimd.wait_ge(SEM["V"], p1a_end["V"])
            INC(nc.gpsimd.dma_start(out=mask_bf.ap().rearrange("p (s q) -> p s q", q=512),
                                    in_=maskI.rearrange("s p q -> p s q")), "LG", 16)
            mask_ready = CNT["LG"]

            base_cnt = dict(CNT)

            def emit_attn(j, emit):
                cnt = dict(base_cnt)

                def inc(instr, s_, k=1):
                    if emit:
                        instr.then_inc(SEM[s_], k)
                    cnt[s_] += k
                    return cnt[s_]

                def w(eng, s_, v):
                    if emit:
                        eng.wait_ge(SEM[s_], v)

                KT_A = 4 * (j + 1)
                KT_B = 4 * (8 - j)
                exp_done = {}
                pv_done = {}
                mask_done = {}
                evac_prev = None
                w(nc.tensor, "LG", mask_ready)
                w(nc.vector, "LG", mask_ready)
                w(nc.scalar, "V", base_cnt["V"])
                for g in range(16):
                    blkp, i_ = g // 4, g % 4
                    for (hh, base) in ((8 * blkp + i_, 0), (8 * blkp + 4 + i_, 64)):
                        kvh = 2 * blkp + (1 if base else 0)
                        qcol = g * 1024
                        for u in range(KT_B + 1):
                            if u < KT_B:
                                par = u % 2
                                if u >= 2:
                                    w(nc.tensor, "SC", exp_done[u - 2])
                                kslice = K_all[base:base + 64, blkp * 4096 + u * 128: blkp * 4096 + (u + 1) * 128]
                                if emit:
                                    if u < KT_A:
                                        nc.tensor.matmul(psS[par][:, 0:512], kslice,
                                                         Q_all[base:base + 64, qcol:qcol + 512],
                                                         start=True, stop=True)
                                        mm = nc.tensor.matmul(psS[par][:, 512:1024], kslice,
                                                              Q_all[base:base + 64, qcol + 512:qcol + 1024],
                                                              start=True, stop=True)
                                    else:
                                        mm = nc.tensor.matmul(psS[par][:, 512:1024], kslice,
                                                              Q_all[base:base + 64, qcol + 512:qcol + 1024],
                                                              start=True, stop=True)
                                else:
                                    mm = None
                                smm = inc(mm, "T")

                                w(nc.scalar, "T", smm)
                                if u >= 2:
                                    w(nc.scalar, "T", pv_done[u - 2])
                                a_diag = KT_A - 4 <= u < KT_A
                                b_diag = KT_B - 4 <= u
                                if emit:
                                    if u < KT_A:
                                        if a_diag:
                                            nc.scalar.activation(PTt[par][:], psS[par][:, 0:512], AF.Exp, bias=consts[:, 0:1])
                                            act = nc.scalar.activation(PT[par][:, 512:1024], psS[par][:, 512:1024],
                                                                       AF.Exp, bias=consts[:, 0:1])
                                        else:
                                            act = nc.scalar.activation(PT[par][:], psS[par][:], AF.Exp, bias=consts[:, 0:1])
                                    else:
                                        if b_diag:
                                            act = nc.scalar.activation(PTt[par][:], psS[par][:, 512:1024],
                                                                       AF.Exp, bias=consts[:, 0:1])
                                        else:
                                            act = nc.scalar.activation(PT[par][:, 512:1024], psS[par][:, 512:1024],
                                                                       AF.Exp, bias=consts[:, 0:1])
                                else:
                                    act = None
                                exp_done[u] = inc(act, "SC")

                                if a_diag or (b_diag and u >= KT_A):
                                    s_idx = u - (KT_A - 4) if a_diag else u - (KT_B - 4)
                                    dst_cols = (0, 512) if a_diag else (512, 1024)
                                    w(nc.vector, "SC", exp_done[u])
                                    if emit:
                                        tt = nc.vector.tensor_tensor(PT[par][:, dst_cols[0]:dst_cols[1]],
                                                                     PTt[par][:],
                                                                     mask_bf[:, s_idx * 512:(s_idx + 1) * 512], MUL)
                                    else:
                                        tt = None
                                    mask_done[u] = inc(tt, "V")

                            if u >= 1:
                                v = u - 1
                                par = v % 2
                                w(nc.tensor, "SC", exp_done[v])
                                if v in mask_done:
                                    w(nc.tensor, "V", mask_done[v])
                                if v == 0 and evac_prev is not None:
                                    w(nc.tensor, "V", evac_prev)
                                vsl = V_tm[:, v * 512 + kvh * 64: v * 512 + (kvh + 1) * 64]
                                mm = None
                                if emit:
                                    if v < KT_A:
                                        nc.tensor.matmul(psO[base:base + 64, 0:512], vsl, PT[par][:, 0:512],
                                                         start=(v == 0), stop=(v == KT_A - 1),
                                                         tile_position=(0, base))
                                        nc.tensor.matmul(psL[(64 - base):(128 - base), 0:512], ones64[:],
                                                         PT[par][:, 0:512],
                                                         start=(v == 0), stop=(v == KT_A - 1),
                                                         tile_position=(0, 64 - base))
                                    nc.tensor.matmul(psO[base:base + 64, 512:1024], vsl, PT[par][:, 512:1024],
                                                     start=(v == 0), stop=(v == KT_B - 1),
                                                     tile_position=(0, base))
                                    mm = nc.tensor.matmul(psL[(64 - base):(128 - base), 512:1024], ones64[:],
                                                          PT[par][:, 512:1024],
                                                          start=(v == 0), stop=(v == KT_B - 1),
                                                          tile_position=(0, 64 - base))
                                pv_done[u - 1] = inc(mm, "T")

                    w(nc.vector, "T", pv_done[KT_B - 1])
                    rec = tt1 = tt2 = None
                    if emit:
                        nc.vector.tensor_copy(lcp[:], psL[:])
                        rec = nc.vector.reciprocal(linv[:], lcp[:])
                        tt1 = nc.vector.tensor_tensor(O_fm[0:64, g * 1024:(g + 1) * 1024],
                                                      psO[0:64, :], linv[64:128, :], MUL)
                        tt2 = nc.vector.tensor_tensor(O_fm[64:128, g * 1024:(g + 1) * 1024],
                                                      psO[64:128, :], linv[0:64, :], MUL)
                    evac_prev = inc(tt2, "V")
                return cnt

            deltas = []
            for j in range(4):
                c_ = emit_attn(j, emit=False)
                deltas.append({k: c_[k] - base_cnt[k] for k in c_})
            maxd = {k: max(d[k] for d in deltas) for k in deltas[0]}

            for body in nc.Switch(engines=[nc.tensor, nc.scalar, nc.vector],
                                  index=[jval, jval, jval], n=4):
                endc = emit_attn(body, emit=True)
                for (eng, s_) in ((nc.tensor, "T"), (nc.scalar, "SC"), (nc.vector, "V")):
                    pad = (base_cnt[s_] + maxd[s_]) - endc[s_]
                    if pad > 0:
                        eng.sem_inc(SEM[s_], pad)
            for k in maxd:
                CNT[k] = base_cnt[k] + maxd[k]
            p2_end = dict(CNT)
            if DEBUG:
                nc.gpsimd.wait_ge(SEM["V"], p2_end["V"])
                INC(nc.gpsimd.dma_start(out=dO, in_=O_fm[:]), "STO", 16)

    with ExitStack() as sh:
        Eh = sh.enter_context
        bounce = [Eh(nc.sbuf_tensor(f"bounce{i}", [128, 512], F32)) for i in range(2)]
        rows = Eh(nc.sbuf_tensor("rows", [1, 2048], F32))
        mB = Eh(nc.sbuf_tensor("mB", [128, 512], F32))
        rB = Eh(nc.sbuf_tensor("rB", [128, 512], F32))
        y1bfh = Eh(nc.sbuf_tensor("y1bfh", [128, 16 * 512], BF16))

        prev_V = p2_end["V"]
        prev_STO = CNT["STO"]
        for half in range(2):
            hc = half * 512
            with ExitStack() as s5, \
                 nc.psum_tensor(f"psAO0_{half}", [128, 512], F32) as psAO0, \
                 nc.psum_tensor(f"psAO1_{half}", [128, 512], F32) as psAO1, \
                 nc.psum_tensor(f"psSt1_{half}", [1, 512], F32) as psSt1, \
                 nc.psum_tensor(f"psSt2_{half}", [1, 512], F32) as psSt2, \
                 nc.psum_tensor(f"psB_{half}", [128, 1024], F32) as psB:
                E5 = s5.enter_context
                y1pre = E5(nc.sbuf_tensor(f"y1pre_{half}", [128, 16 * 512], F32))
                woT_b = [E5(nc.sbuf_tensor(f"woT{i}_{half}", [128, 2048], BF16)) for i in range(2)]
                xqf = [E5(nc.sbuf_tensor(f"xqf{i}_{half}", [128, 512], F32)) for i in range(2)]
                sqs = [E5(nc.sbuf_tensor(f"sqs{i}_{half}", [128, 512], BF16)) for i in range(2)]
                psAO = [psAO0, psAO1]

                ao_done = {}
                pre_done = {}
                st_done = {}
                wo_done = {}
                xq_done = {}
                for m in range(16):
                    par = m % 2
                    if m < 2:
                        nc.gpsimd.wait_ge(SEM["V"], prev_V)
                        nc.sync.wait_ge(SEM["V"], prev_V)
                    if m >= 2:
                        nc.gpsimd.wait_ge(SEM["T"], ao_done[m - 2])
                        nc.sync.wait_ge(SEM["V"], pre_done[m - 2])
                    INC(nc.gpsimd.dma_start(out=woT_b[par].ap().rearrange("p (c j) -> p c j", j=128),
                                            in_=woSW[m]), "LG", 16)
                    wo_done[m] = CNT["LG"]
                    INC(nc.sync.dma_start(out=xqf[par][:], in_=xq[m * 128:(m + 1) * 128, hc:hc + 512]), "LS", 16)
                    xq_done[m] = CNT["LS"]

                    nc.tensor.wait_ge(SEM["LG"], wo_done[m])
                    if m == 0:
                        nc.tensor.wait_ge(SEM["V"], prev_V)
                    if m >= 2:
                        nc.tensor.wait_ge(SEM["V"], pre_done[m - 2])
                    for c in range(16):
                        mm = nc.tensor.matmul(psAO[par][:], woT_b[par][:, c * 128:(c + 1) * 128],
                                              O_fm[:, c * 1024 + hc: c * 1024 + hc + 512],
                                              start=(c == 0), stop=(c == 15))
                    ao_done[m] = INC(mm, "T")

                    nc.vector.wait_ge(SEM["T"], ao_done[m])
                    nc.vector.wait_ge(SEM["LS"], xq_done[m])
                    if m >= 2:
                        nc.vector.wait_ge(SEM["T"], st_done[m - 2])
                    nc.vector.tensor_tensor(y1pre[:, m * 512:(m + 1) * 512], psAO[par][:], xqf[par][:], ADD)
                    nc.vector.tensor_copy(y1bfh[:, m * 512:(m + 1) * 512], y1pre[:, m * 512:(m + 1) * 512])
                    pre_done[m] = INC(nc.vector.tensor_tensor(sqs[par][:], y1bfh[:, m * 512:(m + 1) * 512],
                                                              y1bfh[:, m * 512:(m + 1) * 512], MUL), "V")

                    nc.tensor.wait_ge(SEM["V"], pre_done[m])
                    nc.tensor.matmul(psSt1[:], ones1b[:], y1bfh[:, m * 512:(m + 1) * 512],
                                     start=(m == 0), stop=(m == 15))
                    mm = nc.tensor.matmul(psSt2[:], ones1b[:], sqs[par][:],
                                          start=(m == 0), stop=(m == 15))
                    st_done[m] = INC(mm, "T")

                nc.vector.wait_ge(SEM["T"], st_done[15])
                nc.vector.tensor_scalar_mul(rows[0:1, 0:512], psSt1[:], 1.0 / D)
                nc.vector.tensor_scalar_mul(rows[0:1, 512:1024], psSt2[:], 1.0 / D)
                nc.vector.tensor_tensor(rows[0:1, 1024:1536], rows[0:1, 0:512], rows[0:1, 0:512], MUL)
                vfin = INC(nc.vector.tensor_tensor(rows[0:1, 1536:2048], rows[0:1, 512:1024],
                                                   rows[0:1, 1024:1536], SUB), "V")
                nc.scalar.wait_ge(SEM["V"], vfin)
                nc.scalar.activation(rows[0:1, 1024:1536], rows[0:1, 1536:2048], AF.Ln, bias=consts[0:1, 1:2])
                scf = INC(nc.scalar.activation(rows[0:1, 512:1024], rows[0:1, 1024:1536], AF.Exp, scale=-0.5), "SC")
                nc.tensor.wait_ge(SEM["SC"], scf)
                nc.tensor.matmul(psB[:, 0:512], ones1f[:], rows[0:1, 0:512], start=True, stop=True)
                bts = INC(nc.tensor.matmul(psB[:, 512:1024], ones1f[:], rows[0:1, 512:1024],
                                           start=True, stop=True), "T")
                nc.vector.wait_ge(SEM["T"], bts)
                nc.vector.tensor_copy(mB[:], psB[:, 0:512])
                bev = INC(nc.vector.tensor_copy(rB[:], psB[:, 512:1024]), "V")

                norm_done = {}
                spill_done = {}
                for m in range(16):
                    par = m % 2
                    nc.vector.wait_ge(SEM["T"], bts)
                    if m < 2:
                        nc.vector.wait_ge(SEM["STO"], prev_STO)
                    if m >= 2:
                        nc.vector.wait_ge(SEM["STO"], spill_done[m - 2])
                    nc.vector.tensor_tensor(scrB[:, 0:512], y1pre[:, m * 512:(m + 1) * 512], mB[:], SUB)
                    nc.vector.tensor_tensor(bounce[par][:], scrB[:, 0:512], rB[:], MUL)
                    norm_done[m] = INC(nc.vector.tensor_copy(y1bfh[:, m * 512:(m + 1) * 512], bounce[par][:]), "V")
                    nc.sync.wait_ge(SEM["V"], norm_done[m])
                    spill_done[m] = INC(nc.sync.dma_start(out=y1sp[m * 128:(m + 1) * 128, hc:hc + 512],
                                                          in_=bounce[par][:]), "STO", 16)
                p3_end = dict(CNT)

            with ExitStack() as s6:
                E6 = s6.enter_context
                f_sb = E6(nc.sbuf_tensor(f"f_sb_{half}", [128, 64 * 512], BF16))
                with ExitStack() as s6a, \
                     nc.psum_tensor(f"psG0_{half}", [128, 512], F32) as psG0, \
                     nc.psum_tensor(f"psG1_{half}", [128, 512], F32) as psG1, \
                     nc.psum_tensor(f"psU0_{half}", [128, 512], F32) as psU0, \
                     nc.psum_tensor(f"psU1_{half}", [128, 512], F32) as psU1:
                    E6a = s6a.enter_context
                    wg_b = [E6a(nc.sbuf_tensor(f"wg{i}_{half}", [128, 2048], BF16)) for i in range(2)]
                    wu_b = [E6a(nc.sbuf_tensor(f"wu{i}_{half}", [128, 2048], BF16)) for i in range(2)]
                    gsc = [E6a(nc.sbuf_tensor(f"gsc{i}_{half}", [128, 512], BF16)) for i in range(2)]
                    usc = [E6a(nc.sbuf_tensor(f"usc{i}_{half}", [128, 512], BF16)) for i in range(2)]
                    psG = [psG0, psG1]
                    psU = [psU0, psU1]

                    gu_done = {}
                    silu_done = {}
                    fdone = {}
                    wld = {}
                    for ic in range(64):
                        par = ic % 2
                        if ic < 2:
                            nc.gpsimd.wait_ge(SEM["V"], p3_end["V"])
                        if ic >= 2:
                            nc.gpsimd.wait_ge(SEM["T"], gu_done[ic - 2])
                        INC(nc.gpsimd.dma_start(out=wg_b[par].ap().rearrange("p (c j) -> p c j", j=128),
                                                in_=wgSW[ic]), "LG", 16)
                        INC(nc.gpsimd.dma_start(out=wu_b[par].ap().rearrange("p (c j) -> p c j", j=128),
                                                in_=wuSW[ic]), "LG", 16)
                        wld[ic] = CNT["LG"]

                        nc.tensor.wait_ge(SEM["LG"], wld[ic])
                        if ic == 0:
                            nc.tensor.wait_ge(SEM["V"], p3_end["V"])
                        if ic >= 2:
                            nc.tensor.wait_ge(SEM["SC"], silu_done[ic - 2])
                            nc.tensor.wait_ge(SEM["V"], fdone[ic - 2])
                        for c in range(16):
                            nc.tensor.matmul(psG[par][:], wg_b[par][:, c * 128:(c + 1) * 128],
                                             y1bfh[:, c * 512:(c + 1) * 512],
                                             start=(c == 0), stop=(c == 15))
                        for c in range(16):
                            mm = nc.tensor.matmul(psU[par][:], wu_b[par][:, c * 128:(c + 1) * 128],
                                                  y1bfh[:, c * 512:(c + 1) * 512],
                                                  start=(c == 0), stop=(c == 15))
                        gu_done[ic] = INC(mm, "T")

                        nc.scalar.wait_ge(SEM["T"], gu_done[ic])
                        if ic >= 2:
                            nc.scalar.wait_ge(SEM["V"], fdone[ic - 2])
                        silu_done[ic] = INC(nc.scalar.activation(gsc[par][:], psG[par][:], AF.Silu,
                                                                 scale=sg_sb[:, ic:ic + 1]), "SC")
                        nc.vector.wait_ge(SEM["T"], gu_done[ic])
                        nc.vector.tensor_scalar_mul(usc[par][:], psU[par][:], sup_sb[:, ic:ic + 1])
                        nc.vector.wait_ge(SEM["SC"], silu_done[ic])
                        fdone[ic] = INC(nc.vector.tensor_tensor(f_sb[:, ic * 512:(ic + 1) * 512],
                                                                gsc[par][:], usc[par][:], MUL), "V")
                    p4a_end = dict(CNT)

                with ExitStack() as s6b, \
                     nc.psum_tensor(f"psD0_{half}", [128, 512], F32) as psD0, \
                     nc.psum_tensor(f"psD1_{half}", [128, 512], F32) as psD1, \
                     nc.psum_tensor(f"psSt3_{half}", [1, 512], F32) as psSt3, \
                     nc.psum_tensor(f"psSt4_{half}", [1, 512], F32) as psSt4, \
                     nc.psum_tensor(f"psB2_{half}", [128, 1024], F32) as psB2:
                    E6b = s6b.enter_context
                    wd_b = [E6b(nc.sbuf_tensor(f"wd{i}_{half}", [128, 8192], BF16)) for i in range(2)]
                    y2pre = E6b(nc.sbuf_tensor(f"y2pre_{half}", [128, 16 * 512], F32))
                    y1re = [E6b(nc.sbuf_tensor(f"y1re{i}_{half}", [128, 512], F32)) for i in range(2)]
                    y2bf = [E6b(nc.sbuf_tensor(f"y2bf{i}_{half}", [128, 512], BF16)) for i in range(2)]
                    sq2 = [E6b(nc.sbuf_tensor(f"sq2{i}_{half}", [128, 512], BF16)) for i in range(2)]
                    psD = [psD0, psD1]

                    dmm_done = {}
                    d_ev = {}
                    st2_done = {}
                    wdld = {}
                    y1ld = {}
                    for m2 in range(16):
                        par = m2 % 2
                        if m2 < 2:
                            nc.gpsimd.wait_ge(SEM["V"], p4a_end["V"])
                        if m2 >= 2:
                            nc.gpsimd.wait_ge(SEM["T"], dmm_done[m2 - 2])
                            nc.sync.wait_ge(SEM["V"], d_ev[m2 - 2])
                        INC(nc.gpsimd.dma_start(out=wd_b[par].ap().rearrange("p (c j) -> p c j", j=128),
                                                in_=wdSW[m2]), "LG", 16)
                        wdld[m2] = CNT["LG"]
                        nc.sync.wait_ge(SEM["STO"], p3_end["STO"])
                        INC(nc.sync.dma_start(out=y1re[par][:], in_=y1sp[m2 * 128:(m2 + 1) * 128, hc:hc + 512]),
                            "LS", 16)
                        y1ld[m2] = CNT["LS"]

                        nc.tensor.wait_ge(SEM["LG"], wdld[m2])
                        if m2 == 0:
                            nc.tensor.wait_ge(SEM["V"], p4a_end["V"])
                        if m2 >= 2:
                            nc.tensor.wait_ge(SEM["V"], d_ev[m2 - 2])
                        for ic in range(64):
                            mm = nc.tensor.matmul(psD[par][:], wd_b[par][:, ic * 128:(ic + 1) * 128],
                                                  f_sb[:, ic * 512:(ic + 1) * 512],
                                                  start=(ic == 0), stop=(ic == 63))
                        dmm_done[m2] = INC(mm, "T")

                        nc.vector.wait_ge(SEM["T"], dmm_done[m2])
                        nc.vector.wait_ge(SEM["LS"], y1ld[m2])
                        if m2 >= 2:
                            nc.vector.wait_ge(SEM["T"], st2_done[m2 - 2])
                        nc.vector.tensor_scalar_mul(scrB[:, 0:512], psD[par][:], sd_sb[:, m2:m2 + 1])
                        nc.vector.tensor_tensor(y2pre[:, m2 * 512:(m2 + 1) * 512], scrB[:, 0:512],
                                                y1re[par][:], ADD)
                        nc.vector.tensor_copy(y2bf[par][:], y2pre[:, m2 * 512:(m2 + 1) * 512])
                        d_ev[m2] = INC(nc.vector.tensor_tensor(sq2[par][:], y2bf[par][:],
                                                               y2bf[par][:], MUL), "V")

                        nc.tensor.wait_ge(SEM["V"], d_ev[m2])
                        nc.tensor.matmul(psSt3[:], ones1b[:], y2bf[par][:], start=(m2 == 0), stop=(m2 == 15))
                        mm = nc.tensor.matmul(psSt4[:], ones1b[:], sq2[par][:], start=(m2 == 0), stop=(m2 == 15))
                        st2_done[m2] = INC(mm, "T")

                    nc.vector.wait_ge(SEM["T"], st2_done[15])
                    nc.vector.tensor_scalar_mul(rows[0:1, 0:512], psSt3[:], 1.0 / D)
                    nc.vector.tensor_scalar_mul(rows[0:1, 512:1024], psSt4[:], 1.0 / D)
                    nc.vector.tensor_tensor(rows[0:1, 1024:1536], rows[0:1, 0:512], rows[0:1, 0:512], MUL)
                    vfin = INC(nc.vector.tensor_tensor(rows[0:1, 1536:2048], rows[0:1, 512:1024],
                                                       rows[0:1, 1024:1536], SUB), "V")
                    nc.scalar.wait_ge(SEM["V"], vfin)
                    nc.scalar.activation(rows[0:1, 1024:1536], rows[0:1, 1536:2048], AF.Ln, bias=consts[0:1, 1:2])
                    scf = INC(nc.scalar.activation(rows[0:1, 512:1024], rows[0:1, 1024:1536],
                                                   AF.Exp, scale=-0.5), "SC")
                    nc.tensor.wait_ge(SEM["SC"], scf)
                    nc.tensor.matmul(psB2[:, 0:512], ones1f[:], rows[0:1, 0:512], start=True, stop=True)
                    bts = INC(nc.tensor.matmul(psB2[:, 512:1024], ones1f[:], rows[0:1, 512:1024],
                                               start=True, stop=True), "T")
                    nc.vector.wait_ge(SEM["T"], bts)
                    nc.vector.tensor_copy(mB[:], psB2[:, 0:512])
                    INC(nc.vector.tensor_copy(rB[:], psB2[:, 512:1024]), "V")

                    out_done = {}
                    for m2 in range(16):
                        par = m2 % 2
                        if m2 < 2:
                            nc.vector.wait_ge(SEM["STO"], p3_end["STO"])
                        if m2 >= 2:
                            nc.vector.wait_ge(SEM["STO"], out_done[m2 - 2])
                        nc.vector.tensor_tensor(scrB[:, 512:1024], y2pre[:, m2 * 512:(m2 + 1) * 512],
                                                mB[:], SUB)
                        nv = INC(nc.vector.tensor_tensor(bounce[par][:], scrB[:, 512:1024], rB[:], MUL), "V")
                        nc.sync.wait_ge(SEM["V"], nv)
                        out_done[m2] = INC(nc.sync.dma_start(out=out_fm[m2 * 128:(m2 + 1) * 128, hc:hc + 512],
                                                             in_=bounce[par][:]), "STO", 16)
            prev_V = CNT["V"]
            prev_STO = CNT["STO"]

    nc.sync.wait_ge(SEM["STO"], CNT["STO"])
    return nc


_NC_CACHE = {}


def _get_nc():
    if "nc" not in _NC_CACHE:
        _NC_CACHE["nc"] = build_nc()
    return _NC_CACHE["nc"]


def _host_prep(L0_attn_x, L0_attn_wq, L0_attn_wk, L0_attn_wv, L0_attn_wo,
               L0_ffn_w_gate, L0_ffn_w_up, L0_ffn_w_down,
               L0_ffn_s_gate, L0_ffn_s_up, L0_ffn_s_down):
    f32 = np.float32
    x = np.asarray(L0_attn_x, dtype=f32)
    wq = np.asarray(L0_attn_wq, dtype=f32)
    wk = np.asarray(L0_attn_wk, dtype=f32)
    wv = np.asarray(L0_attn_wv, dtype=f32)
    wo = np.asarray(L0_attn_wo, dtype=f32)
    wg = np.asarray(L0_ffn_w_gate)
    wu = np.asarray(L0_ffn_w_up)
    wd = np.asarray(L0_ffn_w_down)

    perm = []
    for g in range(16):
        blk, i_ = g // 4, g % 4
        for hh in (8 * blk + i_, 8 * blk + 4 + i_):
            perm.extend(range(64 * hh, 64 * hh + 64))
    perm = np.array(perm)

    wqT = np.ascontiguousarray(wq[perm].T)
    wqTs = np.ascontiguousarray(wqT.reshape(2048, 4, 512).transpose(1, 0, 2))
    wkTc = np.ascontiguousarray(wk.T)
    wvTc = np.ascontiguousarray(wv.T)
    woT = np.ascontiguousarray(wo.T[perm])
    woSW = np.ascontiguousarray(woT.reshape(16, 128, 16, 128).transpose(2, 1, 0, 3))
    wgSW = np.ascontiguousarray(wg.reshape(16, 128, 64, 128).transpose(2, 1, 0, 3))
    wuSW = np.ascontiguousarray(wu.reshape(16, 128, 64, 128).transpose(2, 1, 0, 3))
    wdSW = np.ascontiguousarray(wd.reshape(64, 128, 16, 128).transpose(2, 1, 0, 3))
    sgT = np.ascontiguousarray(np.asarray(L0_ffn_s_gate, f32).reshape(64, 128).T)
    supT = np.ascontiguousarray(np.asarray(L0_ffn_s_up, f32).reshape(64, 128).T)
    sdT = np.ascontiguousarray(np.asarray(L0_ffn_s_down, f32).reshape(16, 128).T)

    inv = (1.0 / 10000.0 ** (np.arange(0, 64, 2, dtype=f32) / 64.0)).astype(f32)
    t = np.arange(S, dtype=f32)
    fr = np.outer(t, inv).astype(f32)
    cosf = np.cos(fr).astype(f32)
    sinf = np.sin(fr).astype(f32)
    ck8 = np.ascontiguousarray(np.tile(cosf, (1, 8)))
    sk8 = np.ascontiguousarray(np.tile(sinf, (1, 8)))
    cq8f = (ck8 * np.float32(0.125)).astype(f32)
    sq8f = (sk8 * np.float32(0.125)).astype(f32)

    p_ = np.arange(128)[:, None]
    qi = np.arange(512)[None, :]
    maskI = np.stack([(128 * s_ + p_ <= qi).astype(f32) for s_ in range(4)])

    ident = np.eye(128, dtype=f32)

    xT = [np.ascontiguousarray(x[b].T) for b in range(2)]

    in_maps = []
    for c in range(8):
        b, j = c // 4, c % 4
        acols = slice(512 * j, 512 * j + 512)
        bcols = slice(512 * (7 - j), 512 * (7 - j) + 512)
        xqc = np.ascontiguousarray(np.concatenate([xT[b][:, acols], xT[b][:, bcols]], axis=1))
        cq8 = np.ascontiguousarray(np.concatenate([cq8f[acols.start:acols.stop],
                                                   cq8f[bcols.start:bcols.stop]], axis=0))
        sq8 = np.ascontiguousarray(np.concatenate([sq8f[acols.start:acols.stop],
                                                   sq8f[bcols.start:bcols.stop]], axis=0))
        in_maps.append({
            "xT": xT[b], "xq": xqc,
            "wqTs": wqTs, "wkT": wkTc, "wvT": wvTc, "woSW": woSW,
            "wgSW": wgSW, "wuSW": wuSW, "wdSW": wdSW,
            "sgT": sgT, "supT": supT, "sdT": sdT,
            "ck8": ck8, "sk8": sk8, "cq8": cq8, "sq8": sq8,
            "maskI": maskI, "ident": ident,
        })
    return in_maps


def kernel(**inputs):
    in_maps = _host_prep(**inputs)
    nc = _get_nc()
    res = run_bass_kernel_spmd(nc, in_maps, core_ids=list(range(8)))
    out = np.empty((2, S, D), dtype=np.float32)
    for c in range(8):
        b, j = c // 4, c % 4
        of = np.asarray(res.results[c]["out_fm"])
        out[b, 512 * j:512 * j + 512] = of[:, 0:512].T
        out[b, 512 * (7 - j):512 * (7 - j) + 512] = of[:, 512:1024].T
    return out


# version 05
# speedup vs baseline: 1.1008x; 1.1008x over previous
import numpy as np
import ml_dtypes
from contextlib import ExitStack

import concourse.bass as bass
import concourse.mybir as mybir
from concourse.bass_utils import run_bass_kernel_spmd
from concourse.alu_op_type import AluOpType

F32 = mybir.dt.float32
BF16 = mybir.dt.bfloat16
I8 = mybir.dt.int8
I32 = mybir.dt.int32
AF = mybir.ActivationFunctionType
AX = mybir.AxisListType
ET = mybir.EngineType
MUL = AluOpType.mult
ADD = AluOpType.add
SUB = AluOpType.subtract

D = 2048
S = 4096
NCH = 16
QL = 1024
HD = 64
INTER = 8192
EPS = 1e-5
EXP_SHIFT = -4.0
DEBUG = False


def build_nc():
    nc = bass.Bass("TRN2", target_bir_lowering=False)

    xT = nc.dram_tensor("xT", [D, S], F32, kind="ExternalInput").ap()
    xq = nc.dram_tensor("xq", [D, QL], F32, kind="ExternalInput").ap()
    wqTs = nc.dram_tensor("wqTs", [4, D, 512], F32, kind="ExternalInput").ap()
    wkT = nc.dram_tensor("wkT", [D, 512], F32, kind="ExternalInput").ap()
    wvT = nc.dram_tensor("wvT", [D, 512], F32, kind="ExternalInput").ap()
    woSW = nc.dram_tensor("woSW", [16, 128, 16, 128], F32, kind="ExternalInput").ap()
    wgSW = nc.dram_tensor("wgSW", [64, 128, 16, 128], I8, kind="ExternalInput").ap()
    wuSW = nc.dram_tensor("wuSW", [64, 128, 16, 128], I8, kind="ExternalInput").ap()
    wdSW = nc.dram_tensor("wdSW", [16, 128, 64, 128], I8, kind="ExternalInput").ap()
    sgT = nc.dram_tensor("sgT", [128, 64], F32, kind="ExternalInput").ap()
    supT = nc.dram_tensor("supT", [128, 64], F32, kind="ExternalInput").ap()
    sdT = nc.dram_tensor("sdT", [128, 16], F32, kind="ExternalInput").ap()
    ck8 = nc.dram_tensor("ck8", [S, 256], F32, kind="ExternalInput").ap()
    sk8 = nc.dram_tensor("sk8", [S, 256], F32, kind="ExternalInput").ap()
    cq8 = nc.dram_tensor("cq8", [QL, 256], F32, kind="ExternalInput").ap()
    sq8 = nc.dram_tensor("sq8", [QL, 256], F32, kind="ExternalInput").ap()
    maskI = nc.dram_tensor("maskI", [4, 128, 1024], F32, kind="ExternalInput").ap()
    ident = nc.dram_tensor("ident", [128, 128], F32, kind="ExternalInput").ap()
    out_fm = nc.dram_tensor("out_fm", [D, QL], F32, kind="ExternalOutput").ap()
    if DEBUG:
        dQ = nc.dram_tensor("dQ", [128, 16 * 1024], F32, kind="ExternalOutput").ap()
        dK = nc.dram_tensor("dK", [128, 4 * 4096], F32, kind="ExternalOutput").ap()
        dV = nc.dram_tensor("dV", [128, 32 * 512], F32, kind="ExternalOutput").ap()
        dO = nc.dram_tensor("dO", [128, 16 * 1024], F32, kind="ExternalOutput").ap()
    y1sp = nc.dram_tensor("y1sp", [D, QL], F32, kind="Internal").ap()

    ctx = ExitStack()
    nc._kb_ctx = ctx
    SEM = {}
    CNT = {}
    for s_ in ("LG", "LS", "T", "SC", "V", "STO"):
        SEM[s_] = ctx.enter_context(nc.semaphore(s_))
        CNT[s_] = 0

    def INC(instr, s_, k=1):
        instr.then_inc(SEM[s_], k)
        CNT[s_] += k
        return CNT[s_]

    E = ctx.enter_context
    id_bf = E(nc.sbuf_tensor("id_bf", [128, 128], BF16))
    ones1b = E(nc.sbuf_tensor("ones1b", [128, 1], BF16))
    ones1f = E(nc.sbuf_tensor("ones1f", [1, 128], F32))
    sg_sb = E(nc.sbuf_tensor("sg_sb", [128, 64], F32))
    sup_sb = E(nc.sbuf_tensor("sup_sb", [128, 64], F32))
    sd_sb = E(nc.sbuf_tensor("sd_sb", [128, 16], F32))
    consts = E(nc.sbuf_tensor("consts", [128, 2], F32))
    scrB = E(nc.sbuf_tensor("scrB", [128, 1024], F32))
    O_fm = E(nc.sbuf_tensor("O_fm", [128, 16 * 1024], BF16))

    INC(nc.sync.dma_start(out=sg_sb[:], in_=sgT), "LS", 16)
    INC(nc.sync.dma_start(out=sup_sb[:], in_=supT), "LS", 16)
    INC(nc.sync.dma_start(out=sd_sb[:], in_=sdT), "LS", 16)
    INC(nc.gpsimd.dma_start(out=id_bf[:], in_=ident), "LG", 16)
    nc.vector.memset(consts[:, 0:1], EXP_SHIFT)
    nc.vector.memset(consts[:, 1:2], EPS)
    nc.vector.memset(ones1b[:], 1.0)
    INC(nc.vector.memset(ones1f[:], 1.0), "V", 1)

    jval = nc.partition_id([ET.PE, ET.Activation, ET.DVE]) % 4

    def rope6(psrc, dst, ct, st, scrA):
        g3 = lambda ap: ap.rearrange("p (g c two) -> p g c two", g=8, two=2)
        r_ = g3(psrc)[:, :, :, 0]
        i_ = g3(psrc)[:, :, :, 1]
        c_ = ct.rearrange("p (g c) -> p g c", g=8)
        s_ = st.rearrange("p (g c) -> p g c", g=8)
        t1 = scrA.ap()[:, 0:256].rearrange("p (g c) -> p g c", g=8)
        t2 = scrA.ap()[:, 256:512].rearrange("p (g c) -> p g c", g=8)
        t3 = scrA.ap()[:, 512:768].rearrange("p (g c) -> p g c", g=8)
        t4 = scrA.ap()[:, 768:1024].rearrange("p (g c) -> p g c", g=8)
        nc.vector.tensor_tensor(t1, r_, c_, MUL)
        nc.vector.tensor_tensor(t2, i_, s_, MUL)
        nc.vector.tensor_tensor(t3, r_, s_, MUL)
        nc.vector.tensor_tensor(t4, i_, c_, MUL)
        od = g3(dst)
        nc.vector.tensor_tensor(od[:, :, :, 0], t1, t2, SUB)
        return nc.vector.tensor_tensor(od[:, :, :, 1], t3, t4, ADD)

    with ExitStack() as s1:
        E1 = s1.enter_context
        scrA = E1(nc.sbuf_tensor("scrA", [128, 1024], F32))
        Q_all = E1(nc.sbuf_tensor("Q_all", [128, 16 * 1024], BF16))
        K_all = E1(nc.sbuf_tensor("K_all", [128, 4 * 4096], BF16))
        tabc = [E1(nc.sbuf_tensor(f"tabc{i}", [128, 256], F32)) for i in range(2)]
        tabs = [E1(nc.sbuf_tensor(f"tabs{i}", [128, 256], F32)) for i in range(2)]
        stag = [E1(nc.sbuf_tensor(f"stag{i}", [128, 512], BF16)) for i in range(2)]

        with ExitStack() as s3, \
             nc.psum_tensor("psQ0", [128, 512], F32) as psQ0, \
             nc.psum_tensor("psQ1", [128, 512], F32) as psQ1, \
             nc.psum_tensor("psT", [128, 1024], BF16) as psT:
            E3 = s3.enter_context
            xq_bf = E3(nc.sbuf_tensor("xq_bf", [128, 16 * 1024], BF16))
            wq_sb = E3(nc.sbuf_tensor("wq_sb", [128, 16 * 512], BF16))
            psQ = [psQ0, psQ1]

            xq3 = xq.rearrange("(c p) t -> p c t", p=128)
            xqb3 = xq_bf.ap().rearrange("p (c t) -> p c t", t=1024)
            for hh in range(4):
                INC(nc.gpsimd.dma_start(out=xqb3[:, 4 * hh:4 * hh + 4, :],
                                        in_=xq3[:, 4 * hh:4 * hh + 4, :]), "LG", 16)
            xq_ready = CNT["LG"]

            rope_done = {}
            tpose_done = {}
            evac_done = {}
            mm_done = {}
            tab_done = {}
            wq_done = {}
            wq3 = wq_sb.ap().rearrange("p (c n) -> p c n", n=512)

            for s_ in range(4):
                if s_ > 0:
                    nc.gpsimd.wait_ge(SEM["T"], mm_done[s_ * 8 - 1])
                src = wqTs[s_].rearrange("(c p) n -> p c n", p=128)
                for hh in range(2):
                    INC(nc.gpsimd.dma_start(out=wq3[:, 8 * hh:8 * hh + 8, :],
                                            in_=src[:, 8 * hh:8 * hh + 8, :]), "LG", 16)
                wq_done[s_] = CNT["LG"]
                for tc in range(8):
                    u = s_ * 8 + tc
                    par = u % 2
                    if u >= 2:
                        nc.sync.wait_ge(SEM["V"], rope_done[u - 2])
                    INC(nc.sync.dma_start(out=tabc[par][:], in_=cq8[tc * 128:(tc + 1) * 128, :]), "LS", 16)
                    INC(nc.sync.dma_start(out=tabs[par][:], in_=sq8[tc * 128:(tc + 1) * 128, :]), "LS", 16)
                    tab_done[u] = CNT["LS"]

                    nc.tensor.wait_ge(SEM["LG"], max(xq_ready, wq_done[s_]))
                    if u >= 2:
                        nc.tensor.wait_ge(SEM["V"], rope_done[u - 2])
                    for c in range(NCH):
                        mm = nc.tensor.matmul(psQ[par][:], xq_bf[:, c * 1024 + tc * 128: c * 1024 + (tc + 1) * 128],
                                              wq_sb[:, c * 512:(c + 1) * 512],
                                              start=(c == 0), stop=(c == NCH - 1))
                    mm_done[u] = INC(mm, "T")

                    nc.vector.wait_ge(SEM["T"], mm_done[u])
                    nc.vector.wait_ge(SEM["LS"], tab_done[u])
                    if u >= 2:
                        nc.vector.wait_ge(SEM["T"], tpose_done[u - 2])
                    rope_done[u] = INC(rope6(psQ[par][:], stag[par][:], tabc[par][:], tabs[par][:], scrA), "V")

                    nc.tensor.wait_ge(SEM["V"], rope_done[u])
                    if u >= 2:
                        nc.tensor.wait_ge(SEM["V"], evac_done[u - 2])
                    off = par * 512
                    for p in range(4):
                        tp = nc.tensor.transpose(psT[:, off + p * 128: off + (p + 1) * 128],
                                                 stag[par][:, p * 128:(p + 1) * 128], id_bf[:])
                    tpose_done[u] = INC(tp, "T")

                    nc.vector.wait_ge(SEM["T"], tpose_done[u])
                    dst = Q_all.ap().rearrange("p (g t) -> p g t", t=1024)[:, 4 * s_:4 * s_ + 4, tc * 128:(tc + 1) * 128]
                    src_ = psT.ap()[:, off:off + 512].rearrange("p (g t) -> p g t", t=128)
                    evac_done[u] = INC(nc.vector.tensor_copy(dst, src_), "V")
            p1b_end_V = CNT["V"]
        V_ext = E1(nc.sbuf_tensor("V_ext", [128, 32 * 1024 + 64], BF16))
        with ExitStack() as s2, \
             nc.psum_tensor("psK0", [128, 512], F32) as psK0, \
             nc.psum_tensor("psK1", [128, 512], F32) as psK1, \
             nc.psum_tensor("psTa", [128, 1024], BF16) as psTa:
            E2 = s2.enter_context
            xblk = [E2(nc.sbuf_tensor(f"xblk{i}", [128, 16 * 128], BF16)) for i in range(2)]
            wkv_sb = E2(nc.sbuf_tensor("wkv_sb", [128, 16 * 512], BF16))
            psK = [psK0, psK1]

            vx = V_ext.ap()
            ones_pat = vx[:, 0:32 * 1024].rearrange("p (t k b) -> p t k b", k=8, b=128)[:, :, :, 0:64]
            nc.vector.memset(ones_pat, 1.0)
            vext_ones_V = INC(nc.vector.memset(vx[:, 32 * 1024:32 * 1024 + 64], 1.0), "V")

            xT3 = xT.rearrange("(c p) t -> p c t", p=128)
            xb_done = {}

            def load_xblk(key, wait_t):
                tc = key % 32
                if wait_t is not None:
                    nc.gpsimd.wait_ge(SEM["T"], wait_t)
                d3 = xblk[key % 2].ap().rearrange("p (c t) -> p c t", t=128)
                INC(nc.gpsimd.dma_start(out=d3, in_=xT3[:, :, tc * 128:(tc + 1) * 128]), "LG", 16)
                xb_done[key] = CNT["LG"]

            def load_wkv(w_src, wait_t):
                src_ = w_src.rearrange("(c p) n -> p c n", p=128)
                d3 = wkv_sb.ap().rearrange("p (c n) -> p c n", n=512)
                if wait_t is not None:
                    nc.gpsimd.wait_ge(SEM["T"], wait_t)
                for hh in range(2):
                    INC(nc.gpsimd.dma_start(out=d3[:, 8 * hh:8 * hh + 8, :],
                                            in_=src_[:, 8 * hh:8 * hh + 8, :]), "LG", 16)
                return CNT["LG"]

            nc.gpsimd.wait_ge(SEM["V"], p1b_end_V)
            wk_ready = load_wkv(wkT, None)
            kmm_done = {}
            ropek_done = {}
            ktp_done = {}
            kevac_done = {}
            tabk_done = {}
            load_xblk(0, None)
            load_xblk(1, None)
            for tc in range(32):
                par = tc % 2
                if tc < 2:
                    nc.sync.wait_ge(SEM["V"], p1b_end_V)
                if tc >= 2:
                    nc.sync.wait_ge(SEM["V"], ropek_done[tc - 2])
                INC(nc.sync.dma_start(out=tabc[par][:], in_=ck8[tc * 128:(tc + 1) * 128, :]), "LS", 16)
                INC(nc.sync.dma_start(out=tabs[par][:], in_=sk8[tc * 128:(tc + 1) * 128, :]), "LS", 16)
                tabk_done[tc] = CNT["LS"]

                nc.tensor.wait_ge(SEM["LG"], max(wk_ready, xb_done[tc]))
                if tc < 2:
                    nc.tensor.wait_ge(SEM["V"], p1b_end_V)
                if tc >= 2:
                    nc.tensor.wait_ge(SEM["V"], ropek_done[tc - 2])
                for c in range(NCH):
                    mm = nc.tensor.matmul(psK[par][:], xblk[par][:, c * 128:(c + 1) * 128],
                                          wkv_sb[:, c * 512:(c + 1) * 512],
                                          start=(c == 0), stop=(c == NCH - 1))
                kmm_done[tc] = INC(mm, "T")

                nc.vector.wait_ge(SEM["T"], kmm_done[tc])
                nc.vector.wait_ge(SEM["LS"], tabk_done[tc])
                if tc >= 2:
                    nc.vector.wait_ge(SEM["T"], ktp_done[tc - 2])
                ropek_done[tc] = INC(rope6(psK[par][:], stag[par][:], tabc[par][:], tabs[par][:], scrA), "V")

                nc.tensor.wait_ge(SEM["V"], ropek_done[tc])
                if tc >= 2:
                    nc.tensor.wait_ge(SEM["V"], kevac_done[tc - 2])
                off = par * 512
                for p in range(4):
                    tp = nc.tensor.transpose(psTa[:, off + p * 128: off + (p + 1) * 128],
                                             stag[par][:, p * 128:(p + 1) * 128], id_bf[:])
                ktp_done[tc] = INC(tp, "T")
                nc.vector.wait_ge(SEM["T"], ktp_done[tc])
                dst = K_all.ap().rearrange("p (b t) -> p b t", t=4096)[:, :, tc * 128:(tc + 1) * 128]
                src_ = psTa.ap()[:, off:off + 512].rearrange("p (b t) -> p b t", t=128)
                kevac_done[tc] = INC(nc.vector.tensor_copy(dst, src_), "V")
                if tc + 2 < 32:
                    load_xblk(tc + 2, kmm_done[tc])

            wv_ready = load_wkv(wvT, kmm_done[31])
            vmm_done = {}
            vevac_done = {}
            load_xblk(32 + 0, kmm_done[30])
            load_xblk(32 + 1, kmm_done[31])
            for tc in range(32):
                par = tc % 2
                nc.tensor.wait_ge(SEM["LG"], max(wv_ready, xb_done[32 + tc]))
                if tc < 2:
                    nc.tensor.wait_ge(SEM["V"], vext_ones_V)
                if tc >= 2:
                    nc.tensor.wait_ge(SEM["V"], vevac_done[tc - 2])
                for c in range(NCH):
                    mm = nc.tensor.matmul(psK[par][:], xblk[par][:, c * 128:(c + 1) * 128],
                                          wkv_sb[:, c * 512:(c + 1) * 512],
                                          start=(c == 0), stop=(c == NCH - 1))
                vmm_done[tc] = INC(mm, "T")
                nc.vector.wait_ge(SEM["T"], vmm_done[tc])
                dstv = vx[:, tc * 1024 + 64:(tc + 1) * 1024 + 64].rearrange(
                    "p (k b) -> p k b", b=128)[:, :, 0:64]
                vevac_done[tc] = INC(nc.vector.tensor_copy(dstv, psK[par][:]), "V")
                if tc + 2 < 32:
                    load_xblk(32 + tc + 2, vmm_done[tc])
            p1a_end = dict(CNT)

        with ExitStack() as s4, \
             nc.psum_tensor("psS0", [128, 1024], F32) as psS0, \
             nc.psum_tensor("psS1", [128, 1024], F32) as psS1, \
             nc.psum_tensor("psOlo", [128, 1024], F32) as psOlo, \
             nc.psum_tensor("psOhi", [128, 1024], F32) as psOhi:
            E4 = s4.enter_context
            PT = [E4(nc.sbuf_tensor(f"PT{i}", [128, 1024], BF16)) for i in range(2)]
            PTt = [E4(nc.sbuf_tensor(f"PTt{i}", [128, 1024], BF16)) for i in range(2)]
            linv = E4(nc.sbuf_tensor("linv", [128, 1024], F32))
            lcp = E4(nc.sbuf_tensor("lcp", [128, 1024], F32))
            mask_bf = E4(nc.sbuf_tensor("mask_bf", [128, 4 * 1024], BF16))
            psS = [psS0, psS1]

            nc.gpsimd.wait_ge(SEM["V"], p1a_end["V"])
            INC(nc.gpsimd.dma_start(out=mask_bf.ap().rearrange("p (s q) -> p s q", q=1024),
                                    in_=maskI.rearrange("s p q -> p s q")), "LG", 16)
            mask_ready = CNT["LG"]

            base_cnt = dict(CNT)

            def emit_attn(j, emit):
                cnt = dict(base_cnt)

                def inc(instr, s_, k=1):
                    if emit:
                        instr.then_inc(SEM[s_], k)
                    cnt[s_] += k
                    return cnt[s_]

                def w(eng, s_, v):
                    if emit:
                        eng.wait_ge(SEM[s_], v)

                KT_A = 4 * (j + 1)
                KT_B = 4 * (8 - j)
                exp_done = {}
                pv_done = {}
                mask_done = {}
                evac_half = [None, None]
                gu = 0
                w(nc.tensor, "LG", mask_ready)
                w(nc.vector, "LG", mask_ready)
                w(nc.scalar, "V", base_cnt["V"])
                for g in range(16):
                    blkp = g // 4
                    kvh_lo, kvh_hi = 2 * blkp, 2 * blkp + 1
                    qcol = g * 1024
                    for half in range(2):
                        KT = KT_A if half == 0 else KT_B
                        hq = qcol + half * 512
                        u0 = gu
                        for u in range(KT + 1):
                            if u < KT:
                                uu = u0 + u
                                par = uu % 2
                                if uu >= 2:
                                    w(nc.tensor, "SC", exp_done[uu - 2])
                                kc = blkp * 4096 + u * 128
                                if emit:
                                    nc.tensor.matmul(psS[par][:, 0:512], K_all[0:64, kc:kc + 128],
                                                     Q_all[0:64, hq:hq + 512], start=True, stop=True)
                                    mm = nc.tensor.matmul(psS[par][:, 512:1024], K_all[64:128, kc:kc + 128],
                                                          Q_all[64:128, hq:hq + 512], start=True, stop=True)
                                else:
                                    mm = None
                                smm = inc(mm, "T")

                                w(nc.scalar, "T", smm)
                                if uu >= 2:
                                    w(nc.scalar, "T", pv_done[uu - 2])
                                diag = KT - 4 <= u
                                act = None
                                if emit:
                                    dst_e = PTt[par][:] if diag else PT[par][:]
                                    act = nc.scalar.activation(dst_e, psS[par][:], AF.Exp,
                                                               bias=consts[:, 0:1])
                                exp_done[uu] = inc(act, "SC")
                                if diag:
                                    s_idx = u - (KT - 4)
                                    w(nc.vector, "SC", exp_done[uu])
                                    tt = None
                                    if emit:
                                        tt = nc.vector.tensor_tensor(
                                            PT[par][:], PTt[par][:],
                                            mask_bf[:, s_idx * 1024:(s_idx + 1) * 1024], MUL)
                                    mask_done[uu] = inc(tt, "V")

                            if u >= 1:
                                v_ = u - 1
                                uu = u0 + v_
                                par = uu % 2
                                w(nc.tensor, "SC", exp_done[uu])
                                if uu in mask_done:
                                    w(nc.tensor, "V", mask_done[uu])
                                if v_ == 0 and evac_half[half] is not None:
                                    w(nc.tensor, "V", evac_half[half])
                                hcs = slice(half * 512, half * 512 + 512)
                                win_lo = v_ * 1024 + kvh_lo * 128 + 64
                                win_hi = v_ * 1024 + kvh_hi * 128
                                mm = None
                                if emit:
                                    nc.tensor.matmul(psOlo[:, hcs], V_ext[:, win_lo:win_lo + 128],
                                                     PT[par][:, 0:512],
                                                     start=(v_ == 0), stop=(v_ == KT - 1))
                                    mm = nc.tensor.matmul(psOhi[:, hcs], V_ext[:, win_hi:win_hi + 128],
                                                          PT[par][:, 512:1024],
                                                          start=(v_ == 0), stop=(v_ == KT - 1))
                                pv_done[uu] = inc(mm, "T")
                        gu += KT

                        hcs = slice(half * 512, half * 512 + 512)
                        w(nc.vector, "T", pv_done[gu - 1])
                        tt2 = None
                        if emit:
                            nc.vector.tensor_copy(lcp[64:128, hcs], psOlo[64:128, hcs])
                            nc.vector.tensor_copy(lcp[0:64, hcs], psOhi[0:64, hcs])
                            nc.vector.reciprocal(linv[:, hcs], lcp[:, hcs])
                            nc.vector.tensor_tensor(O_fm[0:64, hq:hq + 512],
                                                    psOlo[0:64, hcs], linv[64:128, hcs], MUL)
                            tt2 = nc.vector.tensor_tensor(O_fm[64:128, hq:hq + 512],
                                                          psOhi[64:128, hcs], linv[0:64, hcs], MUL)
                        evac_half[half] = inc(tt2, "V")
                return cnt

            deltas = []
            for j in range(4):
                c_ = emit_attn(j, emit=False)
                deltas.append({k: c_[k] - base_cnt[k] for k in c_})
            maxd = {k: max(d[k] for d in deltas) for k in deltas[0]}

            for body in nc.Switch(engines=[nc.tensor, nc.scalar, nc.vector],
                                  index=[jval, jval, jval], n=4):
                endc = emit_attn(body, emit=True)
                for (eng, s_) in ((nc.tensor, "T"), (nc.scalar, "SC"), (nc.vector, "V")):
                    pad = (base_cnt[s_] + maxd[s_]) - endc[s_]
                    if pad > 0:
                        eng.sem_inc(SEM[s_], pad)
            for k in maxd:
                CNT[k] = base_cnt[k] + maxd[k]
            p2_end = dict(CNT)

    with ExitStack() as sh:
        Eh = sh.enter_context
        bounce = [Eh(nc.sbuf_tensor(f"bounce{i}", [128, 512], F32)) for i in range(2)]
        rows = Eh(nc.sbuf_tensor("rows", [1, 2048], F32))
        mB = Eh(nc.sbuf_tensor("mB", [128, 512], F32))
        rB = Eh(nc.sbuf_tensor("rB", [128, 512], F32))
        y1bfh = Eh(nc.sbuf_tensor("y1bfh", [128, 16 * 512], BF16))

        prev_V = p2_end["V"]
        prev_STO = CNT["STO"]
        for half in range(2):
            hc = half * 512
            with ExitStack() as s5, \
                 nc.psum_tensor(f"psAO0_{half}", [128, 512], F32) as psAO0, \
                 nc.psum_tensor(f"psAO1_{half}", [128, 512], F32) as psAO1, \
                 nc.psum_tensor(f"psSt1_{half}", [1, 512], F32) as psSt1, \
                 nc.psum_tensor(f"psSt2_{half}", [1, 512], F32) as psSt2, \
                 nc.psum_tensor(f"psB_{half}", [128, 1024], F32) as psB:
                E5 = s5.enter_context
                y1pre = E5(nc.sbuf_tensor(f"y1pre_{half}", [128, 16 * 512], F32))
                woT_b = [E5(nc.sbuf_tensor(f"woT{i}_{half}", [128, 2048], BF16)) for i in range(2)]
                xqf = [E5(nc.sbuf_tensor(f"xqf{i}_{half}", [128, 512], F32)) for i in range(2)]
                sqs = [E5(nc.sbuf_tensor(f"sqs{i}_{half}", [128, 512], BF16)) for i in range(2)]
                psAO = [psAO0, psAO1]

                ao_done = {}
                pre_done = {}
                st_done = {}
                wo_done = {}
                xq_done = {}
                for m in range(16):
                    par = m % 2
                    if m < 2:
                        nc.gpsimd.wait_ge(SEM["V"], prev_V)
                        nc.sync.wait_ge(SEM["V"], prev_V)
                    if m >= 2:
                        nc.gpsimd.wait_ge(SEM["T"], ao_done[m - 2])
                        nc.sync.wait_ge(SEM["V"], pre_done[m - 2])
                    INC(nc.gpsimd.dma_start(out=woT_b[par].ap().rearrange("p (c j) -> p c j", j=128),
                                            in_=woSW[m]), "LG", 16)
                    wo_done[m] = CNT["LG"]
                    INC(nc.sync.dma_start(out=xqf[par][:], in_=xq[m * 128:(m + 1) * 128, hc:hc + 512]), "LS", 16)
                    xq_done[m] = CNT["LS"]

                    nc.tensor.wait_ge(SEM["LG"], wo_done[m])
                    if m == 0:
                        nc.tensor.wait_ge(SEM["V"], prev_V)
                    if m >= 2:
                        nc.tensor.wait_ge(SEM["V"], pre_done[m - 2])
                    for c in range(16):
                        mm = nc.tensor.matmul(psAO[par][:], woT_b[par][:, c * 128:(c + 1) * 128],
                                              O_fm[:, c * 1024 + hc: c * 1024 + hc + 512],
                                              start=(c == 0), stop=(c == 15))
                    ao_done[m] = INC(mm, "T")

                    nc.vector.wait_ge(SEM["T"], ao_done[m])
                    nc.vector.wait_ge(SEM["LS"], xq_done[m])
                    if m >= 2:
                        nc.vector.wait_ge(SEM["T"], st_done[m - 2])
                    nc.vector.tensor_tensor(y1pre[:, m * 512:(m + 1) * 512], psAO[par][:], xqf[par][:], ADD)
                    nc.vector.tensor_copy(y1bfh[:, m * 512:(m + 1) * 512], y1pre[:, m * 512:(m + 1) * 512])
                    pre_done[m] = INC(nc.vector.tensor_tensor(sqs[par][:], y1bfh[:, m * 512:(m + 1) * 512],
                                                              y1bfh[:, m * 512:(m + 1) * 512], MUL), "V")

                    nc.tensor.wait_ge(SEM["V"], pre_done[m])
                    nc.tensor.matmul(psSt1[:], ones1b[:], y1bfh[:, m * 512:(m + 1) * 512],
                                     start=(m == 0), stop=(m == 15))
                    mm = nc.tensor.matmul(psSt2[:], ones1b[:], sqs[par][:],
                                          start=(m == 0), stop=(m == 15))
                    st_done[m] = INC(mm, "T")

                nc.vector.wait_ge(SEM["T"], st_done[15])
                nc.vector.tensor_scalar_mul(rows[0:1, 0:512], psSt1[:], 1.0 / D)
                nc.vector.tensor_scalar_mul(rows[0:1, 512:1024], psSt2[:], 1.0 / D)
                nc.vector.tensor_tensor(rows[0:1, 1024:1536], rows[0:1, 0:512], rows[0:1, 0:512], MUL)
                vfin = INC(nc.vector.tensor_tensor(rows[0:1, 1536:2048], rows[0:1, 512:1024],
                                                   rows[0:1, 1024:1536], SUB), "V")
                nc.scalar.wait_ge(SEM["V"], vfin)
                nc.scalar.activation(rows[0:1, 1024:1536], rows[0:1, 1536:2048], AF.Ln, bias=consts[0:1, 1:2])
                scf = INC(nc.scalar.activation(rows[0:1, 512:1024], rows[0:1, 1024:1536], AF.Exp, scale=-0.5), "SC")
                nc.tensor.wait_ge(SEM["SC"], scf)
                nc.tensor.matmul(psB[:, 0:512], ones1f[:], rows[0:1, 0:512], start=True, stop=True)
                bts = INC(nc.tensor.matmul(psB[:, 512:1024], ones1f[:], rows[0:1, 512:1024],
                                           start=True, stop=True), "T")
                nc.vector.wait_ge(SEM["T"], bts)
                nc.vector.tensor_copy(mB[:], psB[:, 0:512])
                bev = INC(nc.vector.tensor_copy(rB[:], psB[:, 512:1024]), "V")

                norm_done = {}
                spill_done = {}
                for m in range(16):
                    par = m % 2
                    nc.vector.wait_ge(SEM["T"], bts)
                    if m < 2:
                        nc.vector.wait_ge(SEM["STO"], prev_STO)
                    if m >= 2:
                        nc.vector.wait_ge(SEM["STO"], spill_done[m - 2])
                    nc.vector.tensor_tensor(scrB[:, 0:512], y1pre[:, m * 512:(m + 1) * 512], mB[:], SUB)
                    nc.vector.tensor_tensor(bounce[par][:], scrB[:, 0:512], rB[:], MUL)
                    norm_done[m] = INC(nc.vector.tensor_copy(y1bfh[:, m * 512:(m + 1) * 512], bounce[par][:]), "V")
                    nc.sync.wait_ge(SEM["V"], norm_done[m])
                    spill_done[m] = INC(nc.sync.dma_start(out=y1sp[m * 128:(m + 1) * 128, hc:hc + 512],
                                                          in_=bounce[par][:]), "STO", 16)
                p3_end = dict(CNT)

            with ExitStack() as s6:
                E6 = s6.enter_context
                f_sb = E6(nc.sbuf_tensor(f"f_sb_{half}", [128, 64 * 512], BF16))
                with ExitStack() as s6a, \
                     nc.psum_tensor(f"psG0_{half}", [128, 512], F32) as psG0, \
                     nc.psum_tensor(f"psG1_{half}", [128, 512], F32) as psG1, \
                     nc.psum_tensor(f"psU0_{half}", [128, 512], F32) as psU0, \
                     nc.psum_tensor(f"psU1_{half}", [128, 512], F32) as psU1:
                    E6a = s6a.enter_context
                    wg_b = [E6a(nc.sbuf_tensor(f"wg{i}_{half}", [128, 2048], BF16)) for i in range(2)]
                    wu_b = [E6a(nc.sbuf_tensor(f"wu{i}_{half}", [128, 2048], BF16)) for i in range(2)]
                    gsc = [E6a(nc.sbuf_tensor(f"gsc{i}_{half}", [128, 512], BF16)) for i in range(2)]
                    usc = [E6a(nc.sbuf_tensor(f"usc{i}_{half}", [128, 512], BF16)) for i in range(2)]
                    psG = [psG0, psG1]
                    psU = [psU0, psU1]

                    gu_done = {}
                    silu_done = {}
                    fdone = {}
                    wld = {}
                    for ic in range(64):
                        par = ic % 2
                        if ic < 2:
                            nc.gpsimd.wait_ge(SEM["V"], p3_end["V"])
                        if ic >= 2:
                            nc.gpsimd.wait_ge(SEM["T"], gu_done[ic - 2])
                        INC(nc.gpsimd.dma_start(out=wg_b[par].ap().rearrange("p (c j) -> p c j", j=128),
                                                in_=wgSW[ic]), "LG", 16)
                        INC(nc.gpsimd.dma_start(out=wu_b[par].ap().rearrange("p (c j) -> p c j", j=128),
                                                in_=wuSW[ic]), "LG", 16)
                        wld[ic] = CNT["LG"]

                        nc.tensor.wait_ge(SEM["LG"], wld[ic])
                        if ic == 0:
                            nc.tensor.wait_ge(SEM["V"], p3_end["V"])
                        if ic >= 2:
                            nc.tensor.wait_ge(SEM["SC"], silu_done[ic - 2])
                            nc.tensor.wait_ge(SEM["V"], fdone[ic - 2])
                        for c in range(16):
                            nc.tensor.matmul(psG[par][:], wg_b[par][:, c * 128:(c + 1) * 128],
                                             y1bfh[:, c * 512:(c + 1) * 512],
                                             start=(c == 0), stop=(c == 15))
                        for c in range(16):
                            mm = nc.tensor.matmul(psU[par][:], wu_b[par][:, c * 128:(c + 1) * 128],
                                                  y1bfh[:, c * 512:(c + 1) * 512],
                                                  start=(c == 0), stop=(c == 15))
                        gu_done[ic] = INC(mm, "T")

                        nc.scalar.wait_ge(SEM["T"], gu_done[ic])
                        if ic >= 2:
                            nc.scalar.wait_ge(SEM["V"], fdone[ic - 2])
                        silu_done[ic] = INC(nc.scalar.activation(gsc[par][:], psG[par][:], AF.Silu,
                                                                 scale=sg_sb[:, ic:ic + 1]), "SC")
                        nc.vector.wait_ge(SEM["T"], gu_done[ic])
                        nc.vector.tensor_scalar_mul(usc[par][:], psU[par][:], sup_sb[:, ic:ic + 1])
                        nc.vector.wait_ge(SEM["SC"], silu_done[ic])
                        fdone[ic] = INC(nc.vector.tensor_tensor(f_sb[:, ic * 512:(ic + 1) * 512],
                                                                gsc[par][:], usc[par][:], MUL), "V")
                    p4a_end = dict(CNT)

                with ExitStack() as s6b, \
                     nc.psum_tensor(f"psD0_{half}", [128, 512], F32) as psD0, \
                     nc.psum_tensor(f"psD1_{half}", [128, 512], F32) as psD1, \
                     nc.psum_tensor(f"psSt3_{half}", [1, 512], F32) as psSt3, \
                     nc.psum_tensor(f"psSt4_{half}", [1, 512], F32) as psSt4, \
                     nc.psum_tensor(f"psB2_{half}", [128, 1024], F32) as psB2:
                    E6b = s6b.enter_context
                    wd_b = [E6b(nc.sbuf_tensor(f"wd{i}_{half}", [128, 8192], BF16)) for i in range(2)]
                    y2pre = E6b(nc.sbuf_tensor(f"y2pre_{half}", [128, 16 * 512], F32))
                    y1re = [E6b(nc.sbuf_tensor(f"y1re{i}_{half}", [128, 512], F32)) for i in range(2)]
                    y2bf = [E6b(nc.sbuf_tensor(f"y2bf{i}_{half}", [128, 512], BF16)) for i in range(2)]
                    sq2 = [E6b(nc.sbuf_tensor(f"sq2{i}_{half}", [128, 512], BF16)) for i in range(2)]
                    psD = [psD0, psD1]

                    dmm_done = {}
                    d_ev = {}
                    st2_done = {}
                    wdld = {}
                    y1ld = {}
                    for m2 in range(16):
                        par = m2 % 2
                        if m2 < 2:
                            nc.gpsimd.wait_ge(SEM["V"], p4a_end["V"])
                        if m2 >= 2:
                            nc.gpsimd.wait_ge(SEM["T"], dmm_done[m2 - 2])
                            nc.sync.wait_ge(SEM["V"], d_ev[m2 - 2])
                        INC(nc.gpsimd.dma_start(out=wd_b[par].ap().rearrange("p (c j) -> p c j", j=128),
                                                in_=wdSW[m2]), "LG", 16)
                        wdld[m2] = CNT["LG"]
                        nc.sync.wait_ge(SEM["STO"], p3_end["STO"])
                        INC(nc.sync.dma_start(out=y1re[par][:], in_=y1sp[m2 * 128:(m2 + 1) * 128, hc:hc + 512]),
                            "LS", 16)
                        y1ld[m2] = CNT["LS"]

                        nc.tensor.wait_ge(SEM["LG"], wdld[m2])
                        if m2 == 0:
                            nc.tensor.wait_ge(SEM["V"], p4a_end["V"])
                        if m2 >= 2:
                            nc.tensor.wait_ge(SEM["V"], d_ev[m2 - 2])
                        for ic in range(64):
                            mm = nc.tensor.matmul(psD[par][:], wd_b[par][:, ic * 128:(ic + 1) * 128],
                                                  f_sb[:, ic * 512:(ic + 1) * 512],
                                                  start=(ic == 0), stop=(ic == 63))
                        dmm_done[m2] = INC(mm, "T")

                        nc.vector.wait_ge(SEM["T"], dmm_done[m2])
                        nc.vector.wait_ge(SEM["LS"], y1ld[m2])
                        if m2 >= 2:
                            nc.vector.wait_ge(SEM["T"], st2_done[m2 - 2])
                        nc.vector.tensor_scalar_mul(scrB[:, 0:512], psD[par][:], sd_sb[:, m2:m2 + 1])
                        nc.vector.tensor_tensor(y2pre[:, m2 * 512:(m2 + 1) * 512], scrB[:, 0:512],
                                                y1re[par][:], ADD)
                        nc.vector.tensor_copy(y2bf[par][:], y2pre[:, m2 * 512:(m2 + 1) * 512])
                        d_ev[m2] = INC(nc.vector.tensor_tensor(sq2[par][:], y2bf[par][:],
                                                               y2bf[par][:], MUL), "V")

                        nc.tensor.wait_ge(SEM["V"], d_ev[m2])
                        nc.tensor.matmul(psSt3[:], ones1b[:], y2bf[par][:], start=(m2 == 0), stop=(m2 == 15))
                        mm = nc.tensor.matmul(psSt4[:], ones1b[:], sq2[par][:], start=(m2 == 0), stop=(m2 == 15))
                        st2_done[m2] = INC(mm, "T")

                    nc.vector.wait_ge(SEM["T"], st2_done[15])
                    nc.vector.tensor_scalar_mul(rows[0:1, 0:512], psSt3[:], 1.0 / D)
                    nc.vector.tensor_scalar_mul(rows[0:1, 512:1024], psSt4[:], 1.0 / D)
                    nc.vector.tensor_tensor(rows[0:1, 1024:1536], rows[0:1, 0:512], rows[0:1, 0:512], MUL)
                    vfin = INC(nc.vector.tensor_tensor(rows[0:1, 1536:2048], rows[0:1, 512:1024],
                                                       rows[0:1, 1024:1536], SUB), "V")
                    nc.scalar.wait_ge(SEM["V"], vfin)
                    nc.scalar.activation(rows[0:1, 1024:1536], rows[0:1, 1536:2048], AF.Ln, bias=consts[0:1, 1:2])
                    scf = INC(nc.scalar.activation(rows[0:1, 512:1024], rows[0:1, 1024:1536],
                                                   AF.Exp, scale=-0.5), "SC")
                    nc.tensor.wait_ge(SEM["SC"], scf)
                    nc.tensor.matmul(psB2[:, 0:512], ones1f[:], rows[0:1, 0:512], start=True, stop=True)
                    bts = INC(nc.tensor.matmul(psB2[:, 512:1024], ones1f[:], rows[0:1, 512:1024],
                                               start=True, stop=True), "T")
                    nc.vector.wait_ge(SEM["T"], bts)
                    nc.vector.tensor_copy(mB[:], psB2[:, 0:512])
                    INC(nc.vector.tensor_copy(rB[:], psB2[:, 512:1024]), "V")

                    out_done = {}
                    for m2 in range(16):
                        par = m2 % 2
                        if m2 < 2:
                            nc.vector.wait_ge(SEM["STO"], p3_end["STO"])
                        if m2 >= 2:
                            nc.vector.wait_ge(SEM["STO"], out_done[m2 - 2])
                        nc.vector.tensor_tensor(scrB[:, 512:1024], y2pre[:, m2 * 512:(m2 + 1) * 512],
                                                mB[:], SUB)
                        nv = INC(nc.vector.tensor_tensor(bounce[par][:], scrB[:, 512:1024], rB[:], MUL), "V")
                        nc.sync.wait_ge(SEM["V"], nv)
                        out_done[m2] = INC(nc.sync.dma_start(out=out_fm[m2 * 128:(m2 + 1) * 128, hc:hc + 512],
                                                             in_=bounce[par][:]), "STO", 16)
            prev_V = CNT["V"]
            prev_STO = CNT["STO"]

    nc.sync.wait_ge(SEM["STO"], CNT["STO"])
    return nc


_NC_CACHE = {}


def _get_nc():
    if "nc" not in _NC_CACHE:
        _NC_CACHE["nc"] = build_nc()
    return _NC_CACHE["nc"]


def _host_prep(L0_attn_x, L0_attn_wq, L0_attn_wk, L0_attn_wv, L0_attn_wo,
               L0_ffn_w_gate, L0_ffn_w_up, L0_ffn_w_down,
               L0_ffn_s_gate, L0_ffn_s_up, L0_ffn_s_down):
    f32 = np.float32
    x = np.asarray(L0_attn_x, dtype=f32)
    wq = np.asarray(L0_attn_wq, dtype=f32)
    wk = np.asarray(L0_attn_wk, dtype=f32)
    wv = np.asarray(L0_attn_wv, dtype=f32)
    wo = np.asarray(L0_attn_wo, dtype=f32)
    wg = np.asarray(L0_ffn_w_gate)
    wu = np.asarray(L0_ffn_w_up)
    wd = np.asarray(L0_ffn_w_down)

    perm = []
    for g in range(16):
        blk, i_ = g // 4, g % 4
        for hh in (8 * blk + i_, 8 * blk + 4 + i_):
            perm.extend(range(64 * hh, 64 * hh + 64))
    perm = np.array(perm)

    wqT = np.ascontiguousarray(wq[perm].T)
    wqTs = np.ascontiguousarray(wqT.reshape(2048, 4, 512).transpose(1, 0, 2))
    wkTc = np.ascontiguousarray(wk.T)
    wvTc = np.ascontiguousarray(wv.T)
    woT = np.ascontiguousarray(wo.T[perm])
    woSW = np.ascontiguousarray(woT.reshape(16, 128, 16, 128).transpose(2, 1, 0, 3))
    wgSW = np.ascontiguousarray(wg.reshape(16, 128, 64, 128).transpose(2, 1, 0, 3))
    wuSW = np.ascontiguousarray(wu.reshape(16, 128, 64, 128).transpose(2, 1, 0, 3))
    wdSW = np.ascontiguousarray(wd.reshape(64, 128, 16, 128).transpose(2, 1, 0, 3))
    sgT = np.ascontiguousarray(np.asarray(L0_ffn_s_gate, f32).reshape(64, 128).T)
    supT = np.ascontiguousarray(np.asarray(L0_ffn_s_up, f32).reshape(64, 128).T)
    sdT = np.ascontiguousarray(np.asarray(L0_ffn_s_down, f32).reshape(16, 128).T)

    inv = (1.0 / 10000.0 ** (np.arange(0, 64, 2, dtype=f32) / 64.0)).astype(f32)
    t = np.arange(S, dtype=f32)
    fr = np.outer(t, inv).astype(f32)
    cosf = np.cos(fr).astype(f32)
    sinf = np.sin(fr).astype(f32)
    ck8 = np.ascontiguousarray(np.tile(cosf, (1, 8)))
    sk8 = np.ascontiguousarray(np.tile(sinf, (1, 8)))
    cq8f = (ck8 * np.float32(0.125)).astype(f32)
    sq8f = (sk8 * np.float32(0.125)).astype(f32)

    p_ = np.arange(128)[:, None]
    qi = np.arange(512)[None, :]
    maskI = np.stack([np.tile((128 * s_ + p_ <= qi).astype(f32), (1, 2)) for s_ in range(4)])

    ident = np.eye(128, dtype=f32)

    xT = [np.ascontiguousarray(x[b].T) for b in range(2)]

    in_maps = []
    for c in range(8):
        b, j = c // 4, c % 4
        acols = slice(512 * j, 512 * j + 512)
        bcols = slice(512 * (7 - j), 512 * (7 - j) + 512)
        xqc = np.ascontiguousarray(np.concatenate([xT[b][:, acols], xT[b][:, bcols]], axis=1))
        cq8 = np.ascontiguousarray(np.concatenate([cq8f[acols.start:acols.stop],
                                                   cq8f[bcols.start:bcols.stop]], axis=0))
        sq8 = np.ascontiguousarray(np.concatenate([sq8f[acols.start:acols.stop],
                                                   sq8f[bcols.start:bcols.stop]], axis=0))
        in_maps.append({
            "xT": xT[b], "xq": xqc,
            "wqTs": wqTs, "wkT": wkTc, "wvT": wvTc, "woSW": woSW,
            "wgSW": wgSW, "wuSW": wuSW, "wdSW": wdSW,
            "sgT": sgT, "supT": supT, "sdT": sdT,
            "ck8": ck8, "sk8": sk8, "cq8": cq8, "sq8": sq8,
            "maskI": maskI, "ident": ident,
        })
    return in_maps


def kernel(**inputs):
    in_maps = _host_prep(**inputs)
    nc = _get_nc()
    res = run_bass_kernel_spmd(nc, in_maps, core_ids=list(range(8)))
    out = np.empty((2, S, D), dtype=np.float32)
    for c in range(8):
        b, j = c // 4, c % 4
        of = np.asarray(res.results[c]["out_fm"])
        out[b, 512 * j:512 * j + 512] = of[:, 0:512].T
        out[b, 512 * (7 - j):512 * (7 - j) + 512] = of[:, 512:1024].T
    return out


# version 07
# speedup vs baseline: 1.1482x; 1.0431x over previous
import numpy as np
import ml_dtypes
from contextlib import ExitStack

import concourse.bass as bass
import concourse.mybir as mybir
from concourse.bass_utils import run_bass_kernel_spmd
from concourse.alu_op_type import AluOpType

F32 = mybir.dt.float32
BF16 = mybir.dt.bfloat16
I8 = mybir.dt.int8
I32 = mybir.dt.int32
AF = mybir.ActivationFunctionType
AX = mybir.AxisListType
ET = mybir.EngineType
MUL = AluOpType.mult
ADD = AluOpType.add
SUB = AluOpType.subtract

D = 2048
S = 4096
NCH = 16
QL = 1024
HD = 64
INTER = 8192
EPS = 1e-5
EXP_SHIFT = -4.0
DEBUG = False


def build_nc():
    nc = bass.Bass("TRN2", target_bir_lowering=False)

    xT = nc.dram_tensor("xT", [D, S], F32, kind="ExternalInput").ap()
    xq = nc.dram_tensor("xq", [D, QL], F32, kind="ExternalInput").ap()
    wqTs = nc.dram_tensor("wqTs", [4, D, 512], F32, kind="ExternalInput").ap()
    wkT = nc.dram_tensor("wkT", [D, 512], F32, kind="ExternalInput").ap()
    wvT = nc.dram_tensor("wvT", [D, 512], F32, kind="ExternalInput").ap()
    woSW = nc.dram_tensor("woSW", [16, 128, 16, 128], F32, kind="ExternalInput").ap()
    wgSW = nc.dram_tensor("wgSW", [64, 128, 16, 128], I8, kind="ExternalInput").ap()
    wuSW = nc.dram_tensor("wuSW", [64, 128, 16, 128], I8, kind="ExternalInput").ap()
    wdSW = nc.dram_tensor("wdSW", [16, 128, 64, 128], I8, kind="ExternalInput").ap()
    sgT = nc.dram_tensor("sgT", [128, 64], F32, kind="ExternalInput").ap()
    supT = nc.dram_tensor("supT", [128, 64], F32, kind="ExternalInput").ap()
    sdT = nc.dram_tensor("sdT", [128, 16], F32, kind="ExternalInput").ap()
    ck8 = nc.dram_tensor("ck8", [S, 256], F32, kind="ExternalInput").ap()
    sk8 = nc.dram_tensor("sk8", [S, 256], F32, kind="ExternalInput").ap()
    cq8 = nc.dram_tensor("cq8", [QL, 256], F32, kind="ExternalInput").ap()
    sq8 = nc.dram_tensor("sq8", [QL, 256], F32, kind="ExternalInput").ap()
    maskI = nc.dram_tensor("maskI", [4, 128, 1024], F32, kind="ExternalInput").ap()
    ident = nc.dram_tensor("ident", [128, 128], F32, kind="ExternalInput").ap()
    out_fm = nc.dram_tensor("out_fm", [D, QL], F32, kind="ExternalOutput").ap()
    if DEBUG:
        dQ = nc.dram_tensor("dQ", [128, 16 * 1024], F32, kind="ExternalOutput").ap()
        dK = nc.dram_tensor("dK", [128, 4 * 4096], F32, kind="ExternalOutput").ap()
        dV = nc.dram_tensor("dV", [128, 32 * 512], F32, kind="ExternalOutput").ap()
        dO = nc.dram_tensor("dO", [128, 16 * 1024], F32, kind="ExternalOutput").ap()
    y1sp = nc.dram_tensor("y1sp", [D, QL], F32, kind="Internal").ap()

    ctx = ExitStack()
    nc._kb_ctx = ctx
    SEM = {}
    CNT = {}
    for s_ in ("LG", "LS", "T", "SC", "V", "STO"):
        SEM[s_] = ctx.enter_context(nc.semaphore(s_))
        CNT[s_] = 0

    def INC(instr, s_, k=1):
        instr.then_inc(SEM[s_], k)
        CNT[s_] += k
        return CNT[s_]

    E = ctx.enter_context
    id_bf = E(nc.sbuf_tensor("id_bf", [128, 128], BF16))
    ones1b = E(nc.sbuf_tensor("ones1b", [128, 1], BF16))
    ones1f = E(nc.sbuf_tensor("ones1f", [1, 128], F32))
    sg_sb = E(nc.sbuf_tensor("sg_sb", [128, 64], F32))
    sup_sb = E(nc.sbuf_tensor("sup_sb", [128, 64], F32))
    sd_sb = E(nc.sbuf_tensor("sd_sb", [128, 16], F32))
    consts = E(nc.sbuf_tensor("consts", [128, 2], F32))
    scrB = E(nc.sbuf_tensor("scrB", [128, 1024], F32))
    O_fm = E(nc.sbuf_tensor("O_fm", [128, 16 * 1024], BF16))

    INC(nc.sync.dma_start(out=sg_sb[:], in_=sgT), "LS", 16)
    INC(nc.sync.dma_start(out=sup_sb[:], in_=supT), "LS", 16)
    INC(nc.sync.dma_start(out=sd_sb[:], in_=sdT), "LS", 16)
    INC(nc.gpsimd.dma_start(out=id_bf[:], in_=ident), "LG", 16)
    nc.vector.memset(consts[:, 0:1], EXP_SHIFT)
    nc.vector.memset(consts[:, 1:2], EPS)
    nc.vector.memset(ones1b[:], 1.0)
    INC(nc.vector.memset(ones1f[:], 1.0), "V", 1)

    jval = nc.partition_id([ET.PE, ET.Activation, ET.DVE]) % 4

    def rope6(psrc, dst, ct, st, scrA):
        g3 = lambda ap: ap.rearrange("p (g c two) -> p g c two", g=8, two=2)
        r_ = g3(psrc)[:, :, :, 0]
        i_ = g3(psrc)[:, :, :, 1]
        c_ = ct.rearrange("p (g c) -> p g c", g=8)
        s_ = st.rearrange("p (g c) -> p g c", g=8)
        t1 = scrA.ap()[:, 0:256].rearrange("p (g c) -> p g c", g=8)
        t2 = scrA.ap()[:, 256:512].rearrange("p (g c) -> p g c", g=8)
        t3 = scrA.ap()[:, 512:768].rearrange("p (g c) -> p g c", g=8)
        t4 = scrA.ap()[:, 768:1024].rearrange("p (g c) -> p g c", g=8)
        nc.vector.tensor_tensor(t1, r_, c_, MUL)
        nc.vector.tensor_tensor(t2, i_, s_, MUL)
        nc.vector.tensor_tensor(t3, r_, s_, MUL)
        nc.vector.tensor_tensor(t4, i_, c_, MUL)
        od = g3(dst)
        nc.vector.tensor_tensor(od[:, :, :, 0], t1, t2, SUB)
        return nc.vector.tensor_tensor(od[:, :, :, 1], t3, t4, ADD)

    with ExitStack() as s1:
        E1 = s1.enter_context
        scrA = E1(nc.sbuf_tensor("scrA", [128, 1024], F32))
        Q_all = E1(nc.sbuf_tensor("Q_all", [128, 16 * 1024], BF16))
        K_all = E1(nc.sbuf_tensor("K_all", [128, 4 * 4096], BF16))
        tabc = [E1(nc.sbuf_tensor(f"tabc{i}", [128, 256], F32)) for i in range(2)]
        tabs = [E1(nc.sbuf_tensor(f"tabs{i}", [128, 256], F32)) for i in range(2)]
        stag = [E1(nc.sbuf_tensor(f"stag{i}", [128, 512], BF16)) for i in range(2)]

        with ExitStack() as s3, \
             nc.psum_tensor("psQ0", [128, 512], F32) as psQ0, \
             nc.psum_tensor("psQ1", [128, 512], F32) as psQ1, \
             nc.psum_tensor("psT", [128, 1024], BF16) as psT:
            E3 = s3.enter_context
            xq_bf = E3(nc.sbuf_tensor("xq_bf", [128, 16 * 1024], BF16))
            wq_b = [E3(nc.sbuf_tensor(f"wq_sb{i}", [128, 16 * 512], BF16)) for i in range(2)]
            psQ = [psQ0, psQ1]

            xq3 = xq.rearrange("(c p) t -> p c t", p=128)
            xqb3 = xq_bf.ap().rearrange("p (c t) -> p c t", t=1024)
            for hh in range(4):
                INC(nc.gpsimd.dma_start(out=xqb3[:, 4 * hh:4 * hh + 4, :],
                                        in_=xq3[:, 4 * hh:4 * hh + 4, :]), "LG", 16)
            xq_ready = CNT["LG"]

            rope_done = {}
            tpose_done = {}
            evac_done = {}
            mm_done = {}
            tab_done = {}
            wq_done = {}

            def load_wq(s_):
                if s_ >= 2:
                    nc.gpsimd.wait_ge(SEM["T"], mm_done[s_ * 8 - 9])
                srcw = wqTs[s_].rearrange("(c p) n -> p c n", p=128)
                w3 = wq_b[s_ % 2].ap().rearrange("p (c n) -> p c n", n=512)
                for hh in range(2):
                    INC(nc.gpsimd.dma_start(out=w3[:, 8 * hh:8 * hh + 8, :],
                                            in_=srcw[:, 8 * hh:8 * hh + 8, :]), "LG", 16)
                wq_done[s_] = CNT["LG"]

            load_wq(0)
            load_wq(1)

            def q_transpose(u):
                s_, tc = u // 8, u % 8
                par = u % 2
                nc.tensor.wait_ge(SEM["V"], rope_done[u])
                if u >= 2:
                    nc.tensor.wait_ge(SEM["V"], evac_done[u - 2])
                off = par * 512
                for p in range(4):
                    tp = nc.tensor.transpose(psT[:, off + p * 128: off + (p + 1) * 128],
                                             stag[par][:, p * 128:(p + 1) * 128], id_bf[:])
                tpose_done[u] = INC(tp, "T")
                nc.vector.wait_ge(SEM["T"], tpose_done[u])
                dst = Q_all.ap().rearrange("p (g t) -> p g t", t=1024)[:, 4 * s_:4 * s_ + 4, tc * 128:(tc + 1) * 128]
                src_ = psT.ap()[:, off:off + 512].rearrange("p (g t) -> p g t", t=128)
                evac_done[u] = INC(nc.vector.tensor_copy(dst, src_), "V")

            for s_ in range(4):
                for tc in range(8):
                    u = s_ * 8 + tc
                    par = u % 2
                    if u >= 2:
                        nc.sync.wait_ge(SEM["V"], rope_done[u - 2])
                    INC(nc.sync.dma_start(out=tabc[par][:], in_=cq8[tc * 128:(tc + 1) * 128, :]), "LS", 16)
                    INC(nc.sync.dma_start(out=tabs[par][:], in_=sq8[tc * 128:(tc + 1) * 128, :]), "LS", 16)
                    tab_done[u] = CNT["LS"]

                    nc.tensor.wait_ge(SEM["LG"], max(xq_ready, wq_done[s_]))
                    if u >= 2:
                        nc.tensor.wait_ge(SEM["V"], rope_done[u - 2])
                    for c in range(NCH):
                        mm = nc.tensor.matmul(psQ[par][:], xq_bf[:, c * 1024 + tc * 128: c * 1024 + (tc + 1) * 128],
                                              wq_b[s_ % 2][:, c * 512:(c + 1) * 512],
                                              start=(c == 0), stop=(c == NCH - 1))
                    mm_done[u] = INC(mm, "T")

                    if u >= 1:
                        q_transpose(u - 1)

                    nc.vector.wait_ge(SEM["T"], mm_done[u])
                    nc.vector.wait_ge(SEM["LS"], tab_done[u])
                    if u >= 2:
                        nc.vector.wait_ge(SEM["T"], tpose_done[u - 2])
                    rope_done[u] = INC(rope6(psQ[par][:], stag[par][:], tabc[par][:], tabs[par][:], scrA), "V")
                if s_ + 2 < 4:
                    load_wq(s_ + 2)
            q_transpose(31)
            p1b_end_V = CNT["V"]
        V_ext = E1(nc.sbuf_tensor("V_ext", [128, 32 * 1024 + 64], BF16))
        with ExitStack() as s2, \
             nc.psum_tensor("psK0", [128, 512], F32) as psK0, \
             nc.psum_tensor("psK1", [128, 512], F32) as psK1, \
             nc.psum_tensor("psTa", [128, 1024], BF16) as psTa:
            E2 = s2.enter_context
            xblk = [E2(nc.sbuf_tensor(f"xblk{i}", [128, 16 * 128], BF16)) for i in range(2)]
            wkv_sb = E2(nc.sbuf_tensor("wkv_sb", [128, 16 * 512], BF16))
            psK = [psK0, psK1]

            vx = V_ext.ap()
            ones_pat = vx[:, 0:32 * 1024].rearrange("p (t k b) -> p t k b", k=8, b=128)[:, :, :, 0:64]
            nc.vector.memset(ones_pat, 1.0)
            vext_ones_V = INC(nc.vector.memset(vx[:, 32 * 1024:32 * 1024 + 64], 1.0), "V")

            xT3 = xT.rearrange("(c p) t -> p c t", p=128)
            xb_done = {}

            def load_xblk(key, wait_t):
                tc = key % 32
                if wait_t is not None:
                    nc.gpsimd.wait_ge(SEM["T"], wait_t)
                d3 = xblk[key % 2].ap().rearrange("p (c t) -> p c t", t=128)
                INC(nc.gpsimd.dma_start(out=d3, in_=xT3[:, :, tc * 128:(tc + 1) * 128]), "LG", 16)
                xb_done[key] = CNT["LG"]

            def load_wkv(w_src, wait_t):
                src_ = w_src.rearrange("(c p) n -> p c n", p=128)
                d3 = wkv_sb.ap().rearrange("p (c n) -> p c n", n=512)
                if wait_t is not None:
                    nc.gpsimd.wait_ge(SEM["T"], wait_t)
                for hh in range(2):
                    INC(nc.gpsimd.dma_start(out=d3[:, 8 * hh:8 * hh + 8, :],
                                            in_=src_[:, 8 * hh:8 * hh + 8, :]), "LG", 16)
                return CNT["LG"]

            nc.gpsimd.wait_ge(SEM["V"], p1b_end_V)
            wk_ready = load_wkv(wkT, None)
            kmm_done = {}
            ropek_done = {}
            ktp_done = {}
            kevac_done = {}
            tabk_done = {}
            load_xblk(0, None)
            load_xblk(1, None)
            def k_transpose(tc):
                par = tc % 2
                nc.tensor.wait_ge(SEM["V"], ropek_done[tc])
                if tc >= 2:
                    nc.tensor.wait_ge(SEM["V"], kevac_done[tc - 2])
                off = par * 512
                for p in range(4):
                    tp = nc.tensor.transpose(psTa[:, off + p * 128: off + (p + 1) * 128],
                                             stag[par][:, p * 128:(p + 1) * 128], id_bf[:])
                ktp_done[tc] = INC(tp, "T")
                nc.vector.wait_ge(SEM["T"], ktp_done[tc])
                dst = K_all.ap().rearrange("p (b t) -> p b t", t=4096)[:, :, tc * 128:(tc + 1) * 128]
                src_ = psTa.ap()[:, off:off + 512].rearrange("p (b t) -> p b t", t=128)
                kevac_done[tc] = INC(nc.vector.tensor_copy(dst, src_), "V")

            for tc in range(32):
                par = tc % 2
                if tc < 2:
                    nc.sync.wait_ge(SEM["V"], p1b_end_V)
                if tc >= 2:
                    nc.sync.wait_ge(SEM["V"], ropek_done[tc - 2])
                INC(nc.sync.dma_start(out=tabc[par][:], in_=ck8[tc * 128:(tc + 1) * 128, :]), "LS", 16)
                INC(nc.sync.dma_start(out=tabs[par][:], in_=sk8[tc * 128:(tc + 1) * 128, :]), "LS", 16)
                tabk_done[tc] = CNT["LS"]

                nc.tensor.wait_ge(SEM["LG"], max(wk_ready, xb_done[tc]))
                if tc < 2:
                    nc.tensor.wait_ge(SEM["V"], p1b_end_V)
                if tc >= 2:
                    nc.tensor.wait_ge(SEM["V"], ropek_done[tc - 2])
                for c in range(NCH):
                    mm = nc.tensor.matmul(psK[par][:], xblk[par][:, c * 128:(c + 1) * 128],
                                          wkv_sb[:, c * 512:(c + 1) * 512],
                                          start=(c == 0), stop=(c == NCH - 1))
                kmm_done[tc] = INC(mm, "T")

                if tc >= 1:
                    k_transpose(tc - 1)

                nc.vector.wait_ge(SEM["T"], kmm_done[tc])
                nc.vector.wait_ge(SEM["LS"], tabk_done[tc])
                if tc >= 2:
                    nc.vector.wait_ge(SEM["T"], ktp_done[tc - 2])
                ropek_done[tc] = INC(rope6(psK[par][:], stag[par][:], tabc[par][:], tabs[par][:], scrA), "V")

                if tc + 2 < 32:
                    load_xblk(tc + 2, kmm_done[tc])
            k_transpose(31)

            wv_ready = load_wkv(wvT, kmm_done[31])
            vmm_done = {}
            vevac_done = {}
            load_xblk(32 + 0, kmm_done[30])
            load_xblk(32 + 1, kmm_done[31])
            for tc in range(32):
                par = tc % 2
                nc.tensor.wait_ge(SEM["LG"], max(wv_ready, xb_done[32 + tc]))
                if tc < 2:
                    nc.tensor.wait_ge(SEM["V"], vext_ones_V)
                if tc >= 2:
                    nc.tensor.wait_ge(SEM["V"], vevac_done[tc - 2])
                for c in range(NCH):
                    mm = nc.tensor.matmul(psK[par][:], xblk[par][:, c * 128:(c + 1) * 128],
                                          wkv_sb[:, c * 512:(c + 1) * 512],
                                          start=(c == 0), stop=(c == NCH - 1))
                vmm_done[tc] = INC(mm, "T")
                nc.vector.wait_ge(SEM["T"], vmm_done[tc])
                dstv = vx[:, tc * 1024 + 64:(tc + 1) * 1024 + 64].rearrange(
                    "p (k b) -> p k b", b=128)[:, :, 0:64]
                vevac_done[tc] = INC(nc.vector.tensor_copy(dstv, psK[par][:]), "V")
                if tc + 2 < 32:
                    load_xblk(32 + tc + 2, vmm_done[tc])
            p1a_end = dict(CNT)

        with ExitStack() as s4, \
             nc.psum_tensor("psS0", [128, 1024], F32) as psS0, \
             nc.psum_tensor("psS1", [128, 1024], F32) as psS1, \
             nc.psum_tensor("psOlo", [128, 1024], F32) as psOlo, \
             nc.psum_tensor("psOhi", [128, 1024], F32) as psOhi:
            E4 = s4.enter_context
            PT = [E4(nc.sbuf_tensor(f"PT{i}", [128, 1024], BF16)) for i in range(2)]
            PTt = [E4(nc.sbuf_tensor(f"PTt{i}", [128, 1024], BF16)) for i in range(2)]
            linv = E4(nc.sbuf_tensor("linv", [128, 1024], F32))
            lcp = E4(nc.sbuf_tensor("lcp", [128, 1024], F32))
            mask_bf = E4(nc.sbuf_tensor("mask_bf", [128, 4 * 1024], BF16))
            psS = [psS0, psS1]

            nc.gpsimd.wait_ge(SEM["V"], p1a_end["V"])
            INC(nc.gpsimd.dma_start(out=mask_bf.ap().rearrange("p (s q) -> p s q", q=1024),
                                    in_=maskI.rearrange("s p q -> p s q")), "LG", 16)
            mask_ready = CNT["LG"]

            base_cnt = dict(CNT)

            def emit_attn(j, emit):
                cnt = dict(base_cnt)

                def inc(instr, s_, k=1):
                    if emit:
                        instr.then_inc(SEM[s_], k)
                    cnt[s_] += k
                    return cnt[s_]

                def w(eng, s_, v):
                    if emit:
                        eng.wait_ge(SEM[s_], v)

                KT_A = 4 * (j + 1)
                KT_B = 4 * (8 - j)
                exp_done = {}
                pv_done = {}
                mask_done = {}
                evac_half = [None, None]
                gu = 0
                w(nc.tensor, "LG", mask_ready)
                w(nc.vector, "LG", mask_ready)
                w(nc.scalar, "V", base_cnt["V"])
                for g in range(16):
                    blkp = g // 4
                    kvh_lo, kvh_hi = 2 * blkp, 2 * blkp + 1
                    qcol = g * 1024
                    for half in (1, 0):
                        KT = KT_A if half == 0 else KT_B
                        hq = qcol + half * 512
                        u0 = gu
                        for u in range(KT + 1):
                            if u < KT:
                                uu = u0 + u
                                par = uu % 2
                                if uu >= 2:
                                    w(nc.tensor, "SC", exp_done[uu - 2])
                                kc = blkp * 4096 + u * 128
                                if emit:
                                    nc.tensor.matmul(psS[par][:, 0:512], K_all[0:64, kc:kc + 128],
                                                     Q_all[0:64, hq:hq + 512], start=True, stop=True)
                                    mm = nc.tensor.matmul(psS[par][:, 512:1024], K_all[64:128, kc:kc + 128],
                                                          Q_all[64:128, hq:hq + 512], start=True, stop=True)
                                else:
                                    mm = None
                                smm = inc(mm, "T")

                                w(nc.scalar, "T", smm)
                                if uu >= 2:
                                    w(nc.scalar, "T", pv_done[uu - 2])
                                diag = KT - 4 <= u
                                act = None
                                if emit:
                                    dst_e = PTt[par][:] if diag else PT[par][:]
                                    act = nc.scalar.activation(dst_e, psS[par][:], AF.Exp,
                                                               bias=consts[:, 0:1])
                                exp_done[uu] = inc(act, "SC")
                                if diag:
                                    s_idx = u - (KT - 4)
                                    w(nc.vector, "SC", exp_done[uu])
                                    tt = None
                                    if emit:
                                        tt = nc.vector.tensor_tensor(
                                            PT[par][:], PTt[par][:],
                                            mask_bf[:, s_idx * 1024:(s_idx + 1) * 1024], MUL)
                                    mask_done[uu] = inc(tt, "V")

                            if u >= 1:
                                v_ = u - 1
                                uu = u0 + v_
                                par = uu % 2
                                w(nc.tensor, "SC", exp_done[uu])
                                if uu in mask_done:
                                    w(nc.tensor, "V", mask_done[uu])
                                if v_ == 0 and evac_half[half] is not None:
                                    w(nc.tensor, "V", evac_half[half])
                                hcs = slice(half * 512, half * 512 + 512)
                                win_lo = v_ * 1024 + kvh_lo * 128 + 64
                                win_hi = v_ * 1024 + kvh_hi * 128
                                mm = None
                                if emit:
                                    nc.tensor.matmul(psOlo[:, hcs], V_ext[:, win_lo:win_lo + 128],
                                                     PT[par][:, 0:512],
                                                     start=(v_ == 0), stop=(v_ == KT - 1))
                                    mm = nc.tensor.matmul(psOhi[:, hcs], V_ext[:, win_hi:win_hi + 128],
                                                          PT[par][:, 512:1024],
                                                          start=(v_ == 0), stop=(v_ == KT - 1))
                                pv_done[uu] = inc(mm, "T")
                        gu += KT

                        hcs = slice(half * 512, half * 512 + 512)
                        w(nc.vector, "T", pv_done[gu - 1])
                        tt2 = None
                        if emit:
                            nc.vector.tensor_copy(lcp[64:128, hcs], psOlo[64:128, hcs])
                            nc.vector.tensor_copy(lcp[0:64, hcs], psOhi[0:64, hcs])
                            nc.vector.reciprocal(linv[:, hcs], lcp[:, hcs])
                            nc.vector.tensor_tensor(O_fm[0:64, hq:hq + 512],
                                                    psOlo[0:64, hcs], linv[64:128, hcs], MUL)
                            tt2 = nc.vector.tensor_tensor(O_fm[64:128, hq:hq + 512],
                                                          psOhi[64:128, hcs], linv[0:64, hcs], MUL)
                        evac_half[half] = inc(tt2, "V")
                return cnt

            deltas = []
            for j in range(4):
                c_ = emit_attn(j, emit=False)
                deltas.append({k: c_[k] - base_cnt[k] for k in c_})
            maxd = {k: max(d[k] for d in deltas) for k in deltas[0]}

            for body in nc.Switch(engines=[nc.tensor, nc.scalar, nc.vector],
                                  index=[jval, jval, jval], n=4):
                endc = emit_attn(body, emit=True)
                for (eng, s_) in ((nc.tensor, "T"), (nc.scalar, "SC"), (nc.vector, "V")):
                    pad = (base_cnt[s_] + maxd[s_]) - endc[s_]
                    if pad > 0:
                        eng.sem_inc(SEM[s_], pad)
            for k in maxd:
                CNT[k] = base_cnt[k] + maxd[k]
            p2_end = dict(CNT)

    with ExitStack() as sh:
        Eh = sh.enter_context
        bounce = [Eh(nc.sbuf_tensor(f"bounce{i}", [128, 512], F32)) for i in range(2)]
        rows = Eh(nc.sbuf_tensor("rows", [1, 2048], F32))
        mB = Eh(nc.sbuf_tensor("mB", [128, 512], F32))
        rB = Eh(nc.sbuf_tensor("rB", [128, 512], F32))
        y1bfh = Eh(nc.sbuf_tensor("y1bfh", [128, 16 * 512], BF16))

        prev_V = p2_end["V"]
        prev_STO = CNT["STO"]
        for half in range(2):
            hc = half * 512
            with ExitStack() as s5, \
                 nc.psum_tensor(f"psAO0_{half}", [128, 512], F32) as psAO0, \
                 nc.psum_tensor(f"psAO1_{half}", [128, 512], F32) as psAO1, \
                 nc.psum_tensor(f"psSt1_{half}", [1, 512], F32) as psSt1, \
                 nc.psum_tensor(f"psSt2_{half}", [1, 512], F32) as psSt2, \
                 nc.psum_tensor(f"psB_{half}", [128, 1024], F32) as psB:
                E5 = s5.enter_context
                y1pre = E5(nc.sbuf_tensor(f"y1pre_{half}", [128, 16 * 512], F32))
                woT_b = [E5(nc.sbuf_tensor(f"woT{i}_{half}", [128, 2048], BF16)) for i in range(2)]
                xqf = [E5(nc.sbuf_tensor(f"xqf{i}_{half}", [128, 512], F32)) for i in range(2)]
                sqs = [E5(nc.sbuf_tensor(f"sqs{i}_{half}", [128, 512], BF16)) for i in range(2)]
                psAO = [psAO0, psAO1]

                ao_done = {}
                pre_done = {}
                st_done = {}
                wo_done = {}
                xq_done = {}
                for m in range(16):
                    par = m % 2
                    if m < 2:
                        nc.gpsimd.wait_ge(SEM["V"], prev_V)
                        nc.sync.wait_ge(SEM["V"], prev_V)
                    if m >= 2:
                        nc.gpsimd.wait_ge(SEM["T"], ao_done[m - 2])
                        nc.sync.wait_ge(SEM["V"], pre_done[m - 2])
                    INC(nc.gpsimd.dma_start(out=woT_b[par].ap().rearrange("p (c j) -> p c j", j=128),
                                            in_=woSW[m]), "LG", 16)
                    wo_done[m] = CNT["LG"]
                    INC(nc.sync.dma_start(out=xqf[par][:], in_=xq[m * 128:(m + 1) * 128, hc:hc + 512]), "LS", 16)
                    xq_done[m] = CNT["LS"]

                    nc.tensor.wait_ge(SEM["LG"], wo_done[m])
                    if m == 0:
                        nc.tensor.wait_ge(SEM["V"], prev_V)
                    if m >= 2:
                        nc.tensor.wait_ge(SEM["V"], pre_done[m - 2])
                    for c in range(16):
                        mm = nc.tensor.matmul(psAO[par][:], woT_b[par][:, c * 128:(c + 1) * 128],
                                              O_fm[:, c * 1024 + hc: c * 1024 + hc + 512],
                                              start=(c == 0), stop=(c == 15))
                    ao_done[m] = INC(mm, "T")

                    nc.vector.wait_ge(SEM["T"], ao_done[m])
                    nc.vector.wait_ge(SEM["LS"], xq_done[m])
                    if m >= 2:
                        nc.vector.wait_ge(SEM["T"], st_done[m - 2])
                    nc.vector.tensor_tensor(y1pre[:, m * 512:(m + 1) * 512], psAO[par][:], xqf[par][:], ADD)
                    nc.vector.tensor_copy(y1bfh[:, m * 512:(m + 1) * 512], y1pre[:, m * 512:(m + 1) * 512])
                    pre_done[m] = INC(nc.vector.tensor_tensor(sqs[par][:], y1bfh[:, m * 512:(m + 1) * 512],
                                                              y1bfh[:, m * 512:(m + 1) * 512], MUL), "V")

                    nc.tensor.wait_ge(SEM["V"], pre_done[m])
                    nc.tensor.matmul(psSt1[:], ones1b[:], y1bfh[:, m * 512:(m + 1) * 512],
                                     start=(m == 0), stop=(m == 15))
                    mm = nc.tensor.matmul(psSt2[:], ones1b[:], sqs[par][:],
                                          start=(m == 0), stop=(m == 15))
                    st_done[m] = INC(mm, "T")

                nc.vector.wait_ge(SEM["T"], st_done[15])
                nc.vector.tensor_scalar_mul(rows[0:1, 0:512], psSt1[:], 1.0 / D)
                nc.vector.tensor_scalar_mul(rows[0:1, 512:1024], psSt2[:], 1.0 / D)
                nc.vector.tensor_tensor(rows[0:1, 1024:1536], rows[0:1, 0:512], rows[0:1, 0:512], MUL)
                vfin = INC(nc.vector.tensor_tensor(rows[0:1, 1536:2048], rows[0:1, 512:1024],
                                                   rows[0:1, 1024:1536], SUB), "V")
                nc.scalar.wait_ge(SEM["V"], vfin)
                nc.scalar.activation(rows[0:1, 1024:1536], rows[0:1, 1536:2048], AF.Ln, bias=consts[0:1, 1:2])
                scf = INC(nc.scalar.activation(rows[0:1, 512:1024], rows[0:1, 1024:1536], AF.Exp, scale=-0.5), "SC")
                nc.tensor.wait_ge(SEM["SC"], scf)
                nc.tensor.matmul(psB[:, 0:512], ones1f[:], rows[0:1, 0:512], start=True, stop=True)
                bts = INC(nc.tensor.matmul(psB[:, 512:1024], ones1f[:], rows[0:1, 512:1024],
                                           start=True, stop=True), "T")
                nc.vector.wait_ge(SEM["T"], bts)
                nc.vector.tensor_copy(mB[:], psB[:, 0:512])
                bev = INC(nc.vector.tensor_copy(rB[:], psB[:, 512:1024]), "V")

                norm_done = {}
                spill_done = {}
                cast_done = {}
                for m in range(16):
                    par = m % 2
                    nc.vector.wait_ge(SEM["T"], bts)
                    if m < 2:
                        nc.vector.wait_ge(SEM["STO"], prev_STO)
                    if m >= 2:
                        nc.vector.wait_ge(SEM["STO"], spill_done[m - 2])
                    if m >= 2:
                        nc.vector.wait_ge(SEM["SC"], cast_done[m - 2])
                    nc.vector.tensor_tensor(scrB[:, 0:512], y1pre[:, m * 512:(m + 1) * 512], mB[:], SUB)
                    norm_done[m] = INC(nc.vector.tensor_tensor(bounce[par][:], scrB[:, 0:512], rB[:], MUL), "V")
                    nc.scalar.wait_ge(SEM["V"], norm_done[m])
                    cast_done[m] = INC(nc.scalar.copy(y1bfh[:, m * 512:(m + 1) * 512], bounce[par][:]), "SC")
                    nc.sync.wait_ge(SEM["V"], norm_done[m])
                    spill_done[m] = INC(nc.sync.dma_start(out=y1sp[m * 128:(m + 1) * 128, hc:hc + 512],
                                                          in_=bounce[par][:]), "STO", 16)
                p3_end = dict(CNT)

            with ExitStack() as s6:
                E6 = s6.enter_context
                f_sb = E6(nc.sbuf_tensor(f"f_sb_{half}", [128, 64 * 512], BF16))
                with ExitStack() as s6a, \
                     nc.psum_tensor(f"psG0_{half}", [128, 512], F32) as psG0, \
                     nc.psum_tensor(f"psG1_{half}", [128, 512], F32) as psG1, \
                     nc.psum_tensor(f"psU0_{half}", [128, 512], F32) as psU0, \
                     nc.psum_tensor(f"psU1_{half}", [128, 512], F32) as psU1:
                    E6a = s6a.enter_context
                    wg_b = [E6a(nc.sbuf_tensor(f"wg{i}_{half}", [128, 2048], BF16)) for i in range(2)]
                    wu_b = [E6a(nc.sbuf_tensor(f"wu{i}_{half}", [128, 2048], BF16)) for i in range(2)]
                    gsc = [E6a(nc.sbuf_tensor(f"gsc{i}_{half}", [128, 512], BF16)) for i in range(2)]
                    usc = [E6a(nc.sbuf_tensor(f"usc{i}_{half}", [128, 512], BF16)) for i in range(2)]
                    psG = [psG0, psG1]
                    psU = [psU0, psU1]

                    gu_done = {}
                    silu_done = {}
                    fdone = {}
                    wld = {}
                    for ic in range(64):
                        par = ic % 2
                        if ic < 2:
                            nc.gpsimd.wait_ge(SEM["V"], p3_end["V"])
                        if ic >= 2:
                            nc.gpsimd.wait_ge(SEM["T"], gu_done[ic - 2])
                        INC(nc.gpsimd.dma_start(out=wg_b[par].ap().rearrange("p (c j) -> p c j", j=128),
                                                in_=wgSW[ic]), "LG", 16)
                        INC(nc.gpsimd.dma_start(out=wu_b[par].ap().rearrange("p (c j) -> p c j", j=128),
                                                in_=wuSW[ic]), "LG", 16)
                        wld[ic] = CNT["LG"]

                        nc.tensor.wait_ge(SEM["LG"], wld[ic])
                        if ic == 0:
                            nc.tensor.wait_ge(SEM["V"], p3_end["V"])
                            nc.tensor.wait_ge(SEM["SC"], p3_end["SC"])
                        if ic >= 2:
                            nc.tensor.wait_ge(SEM["SC"], silu_done[ic - 2])
                            nc.tensor.wait_ge(SEM["V"], fdone[ic - 2])
                        for c in range(16):
                            nc.tensor.matmul(psG[par][:], wg_b[par][:, c * 128:(c + 1) * 128],
                                             y1bfh[:, c * 512:(c + 1) * 512],
                                             start=(c == 0), stop=(c == 15))
                        for c in range(16):
                            mm = nc.tensor.matmul(psU[par][:], wu_b[par][:, c * 128:(c + 1) * 128],
                                                  y1bfh[:, c * 512:(c + 1) * 512],
                                                  start=(c == 0), stop=(c == 15))
                        gu_done[ic] = INC(mm, "T")

                        nc.scalar.wait_ge(SEM["T"], gu_done[ic])
                        if ic >= 2:
                            nc.scalar.wait_ge(SEM["V"], fdone[ic - 2])
                        silu_done[ic] = INC(nc.scalar.activation(gsc[par][:], psG[par][:], AF.Silu,
                                                                 scale=sg_sb[:, ic:ic + 1]), "SC")
                        nc.vector.wait_ge(SEM["T"], gu_done[ic])
                        nc.vector.tensor_scalar_mul(usc[par][:], psU[par][:], sup_sb[:, ic:ic + 1])
                        nc.vector.wait_ge(SEM["SC"], silu_done[ic])
                        fdone[ic] = INC(nc.vector.tensor_tensor(f_sb[:, ic * 512:(ic + 1) * 512],
                                                                gsc[par][:], usc[par][:], MUL), "V")
                    p4a_end = dict(CNT)

                with ExitStack() as s6b, \
                     nc.psum_tensor(f"psD0_{half}", [128, 512], F32) as psD0, \
                     nc.psum_tensor(f"psD1_{half}", [128, 512], F32) as psD1, \
                     nc.psum_tensor(f"psSt3_{half}", [1, 512], F32) as psSt3, \
                     nc.psum_tensor(f"psSt4_{half}", [1, 512], F32) as psSt4, \
                     nc.psum_tensor(f"psB2_{half}", [128, 1024], F32) as psB2:
                    E6b = s6b.enter_context
                    wd_b = [E6b(nc.sbuf_tensor(f"wd{i}_{half}", [128, 8192], BF16)) for i in range(2)]
                    y2pre = E6b(nc.sbuf_tensor(f"y2pre_{half}", [128, 16 * 512], F32))
                    y1re = [E6b(nc.sbuf_tensor(f"y1re{i}_{half}", [128, 512], F32)) for i in range(2)]
                    y2bf = [E6b(nc.sbuf_tensor(f"y2bf{i}_{half}", [128, 512], BF16)) for i in range(2)]
                    sq2 = [E6b(nc.sbuf_tensor(f"sq2{i}_{half}", [128, 512], BF16)) for i in range(2)]
                    psD = [psD0, psD1]

                    dmm_done = {}
                    d_ev = {}
                    st2_done = {}
                    wdld = {}
                    y1ld = {}
                    for m2 in range(16):
                        par = m2 % 2
                        if m2 < 2:
                            nc.gpsimd.wait_ge(SEM["V"], p4a_end["V"])
                        if m2 >= 2:
                            nc.gpsimd.wait_ge(SEM["T"], dmm_done[m2 - 2])
                            nc.sync.wait_ge(SEM["V"], d_ev[m2 - 2])
                        INC(nc.gpsimd.dma_start(out=wd_b[par].ap().rearrange("p (c j) -> p c j", j=128),
                                                in_=wdSW[m2]), "LG", 16)
                        wdld[m2] = CNT["LG"]
                        nc.sync.wait_ge(SEM["STO"], p3_end["STO"])
                        INC(nc.sync.dma_start(out=y1re[par][:], in_=y1sp[m2 * 128:(m2 + 1) * 128, hc:hc + 512]),
                            "LS", 16)
                        y1ld[m2] = CNT["LS"]

                        nc.tensor.wait_ge(SEM["LG"], wdld[m2])
                        if m2 == 0:
                            nc.tensor.wait_ge(SEM["V"], p4a_end["V"])
                        if m2 >= 2:
                            nc.tensor.wait_ge(SEM["V"], d_ev[m2 - 2])
                        for ic in range(64):
                            mm = nc.tensor.matmul(psD[par][:], wd_b[par][:, ic * 128:(ic + 1) * 128],
                                                  f_sb[:, ic * 512:(ic + 1) * 512],
                                                  start=(ic == 0), stop=(ic == 63))
                        dmm_done[m2] = INC(mm, "T")

                        nc.vector.wait_ge(SEM["T"], dmm_done[m2])
                        nc.vector.wait_ge(SEM["LS"], y1ld[m2])
                        if m2 >= 2:
                            nc.vector.wait_ge(SEM["T"], st2_done[m2 - 2])
                        nc.vector.tensor_scalar_mul(scrB[:, 0:512], psD[par][:], sd_sb[:, m2:m2 + 1])
                        nc.vector.tensor_tensor(y2pre[:, m2 * 512:(m2 + 1) * 512], scrB[:, 0:512],
                                                y1re[par][:], ADD)
                        nc.vector.tensor_copy(y2bf[par][:], y2pre[:, m2 * 512:(m2 + 1) * 512])
                        d_ev[m2] = INC(nc.vector.tensor_tensor(sq2[par][:], y2bf[par][:],
                                                               y2bf[par][:], MUL), "V")

                        nc.tensor.wait_ge(SEM["V"], d_ev[m2])
                        nc.tensor.matmul(psSt3[:], ones1b[:], y2bf[par][:], start=(m2 == 0), stop=(m2 == 15))
                        mm = nc.tensor.matmul(psSt4[:], ones1b[:], sq2[par][:], start=(m2 == 0), stop=(m2 == 15))
                        st2_done[m2] = INC(mm, "T")

                    nc.vector.wait_ge(SEM["T"], st2_done[15])
                    nc.vector.tensor_scalar_mul(rows[0:1, 0:512], psSt3[:], 1.0 / D)
                    nc.vector.tensor_scalar_mul(rows[0:1, 512:1024], psSt4[:], 1.0 / D)
                    nc.vector.tensor_tensor(rows[0:1, 1024:1536], rows[0:1, 0:512], rows[0:1, 0:512], MUL)
                    vfin = INC(nc.vector.tensor_tensor(rows[0:1, 1536:2048], rows[0:1, 512:1024],
                                                       rows[0:1, 1024:1536], SUB), "V")
                    nc.scalar.wait_ge(SEM["V"], vfin)
                    nc.scalar.activation(rows[0:1, 1024:1536], rows[0:1, 1536:2048], AF.Ln, bias=consts[0:1, 1:2])
                    scf = INC(nc.scalar.activation(rows[0:1, 512:1024], rows[0:1, 1024:1536],
                                                   AF.Exp, scale=-0.5), "SC")
                    nc.tensor.wait_ge(SEM["SC"], scf)
                    nc.tensor.matmul(psB2[:, 0:512], ones1f[:], rows[0:1, 0:512], start=True, stop=True)
                    bts = INC(nc.tensor.matmul(psB2[:, 512:1024], ones1f[:], rows[0:1, 512:1024],
                                               start=True, stop=True), "T")
                    nc.vector.wait_ge(SEM["T"], bts)
                    nc.vector.tensor_copy(mB[:], psB2[:, 0:512])
                    INC(nc.vector.tensor_copy(rB[:], psB2[:, 512:1024]), "V")

                    out_done = {}
                    for m2 in range(16):
                        par = m2 % 2
                        if m2 < 2:
                            nc.vector.wait_ge(SEM["STO"], p3_end["STO"])
                        if m2 >= 2:
                            nc.vector.wait_ge(SEM["STO"], out_done[m2 - 2])
                        nc.vector.tensor_tensor(scrB[:, 512:1024], y2pre[:, m2 * 512:(m2 + 1) * 512],
                                                mB[:], SUB)
                        nv = INC(nc.vector.tensor_tensor(bounce[par][:], scrB[:, 512:1024], rB[:], MUL), "V")
                        nc.sync.wait_ge(SEM["V"], nv)
                        out_done[m2] = INC(nc.sync.dma_start(out=out_fm[m2 * 128:(m2 + 1) * 128, hc:hc + 512],
                                                             in_=bounce[par][:]), "STO", 16)
            prev_V = CNT["V"]
            prev_STO = CNT["STO"]

    nc.sync.wait_ge(SEM["STO"], CNT["STO"])
    return nc


_NC_CACHE = {}


def _get_nc():
    if "nc" not in _NC_CACHE:
        _NC_CACHE["nc"] = build_nc()
    return _NC_CACHE["nc"]


def _host_prep(L0_attn_x, L0_attn_wq, L0_attn_wk, L0_attn_wv, L0_attn_wo,
               L0_ffn_w_gate, L0_ffn_w_up, L0_ffn_w_down,
               L0_ffn_s_gate, L0_ffn_s_up, L0_ffn_s_down):
    f32 = np.float32
    x = np.asarray(L0_attn_x, dtype=f32)
    wq = np.asarray(L0_attn_wq, dtype=f32)
    wk = np.asarray(L0_attn_wk, dtype=f32)
    wv = np.asarray(L0_attn_wv, dtype=f32)
    wo = np.asarray(L0_attn_wo, dtype=f32)
    wg = np.asarray(L0_ffn_w_gate)
    wu = np.asarray(L0_ffn_w_up)
    wd = np.asarray(L0_ffn_w_down)

    perm = []
    for g in range(16):
        blk, i_ = g // 4, g % 4
        for hh in (8 * blk + i_, 8 * blk + 4 + i_):
            perm.extend(range(64 * hh, 64 * hh + 64))
    perm = np.array(perm)

    wqT = np.ascontiguousarray(wq[perm].T)
    wqTs = np.ascontiguousarray(wqT.reshape(2048, 4, 512).transpose(1, 0, 2))
    wkTc = np.ascontiguousarray(wk.T)
    wvTc = np.ascontiguousarray(wv.T)
    woT = np.ascontiguousarray(wo.T[perm])
    woSW = np.ascontiguousarray(woT.reshape(16, 128, 16, 128).transpose(2, 1, 0, 3))
    wgSW = np.ascontiguousarray(wg.reshape(16, 128, 64, 128).transpose(2, 1, 0, 3))
    wuSW = np.ascontiguousarray(wu.reshape(16, 128, 64, 128).transpose(2, 1, 0, 3))
    wdSW = np.ascontiguousarray(wd.reshape(64, 128, 16, 128).transpose(2, 1, 0, 3))
    sgT = np.ascontiguousarray(np.asarray(L0_ffn_s_gate, f32).reshape(64, 128).T)
    supT = np.ascontiguousarray(np.asarray(L0_ffn_s_up, f32).reshape(64, 128).T)
    sdT = np.ascontiguousarray(np.asarray(L0_ffn_s_down, f32).reshape(16, 128).T)

    inv = (1.0 / 10000.0 ** (np.arange(0, 64, 2, dtype=f32) / 64.0)).astype(f32)
    t = np.arange(S, dtype=f32)
    fr = np.outer(t, inv).astype(f32)
    cosf = np.cos(fr).astype(f32)
    sinf = np.sin(fr).astype(f32)
    ck8 = np.ascontiguousarray(np.tile(cosf, (1, 8)))
    sk8 = np.ascontiguousarray(np.tile(sinf, (1, 8)))
    cq8f = (ck8 * np.float32(0.125)).astype(f32)
    sq8f = (sk8 * np.float32(0.125)).astype(f32)

    p_ = np.arange(128)[:, None]
    qi = np.arange(512)[None, :]
    maskI = np.stack([np.tile((128 * s_ + p_ <= qi).astype(f32), (1, 2)) for s_ in range(4)])

    ident = np.eye(128, dtype=f32)

    xT = [np.ascontiguousarray(x[b].T) for b in range(2)]

    in_maps = []
    for c in range(8):
        b, j = c // 4, c % 4
        acols = slice(512 * j, 512 * j + 512)
        bcols = slice(512 * (7 - j), 512 * (7 - j) + 512)
        xqc = np.ascontiguousarray(np.concatenate([xT[b][:, acols], xT[b][:, bcols]], axis=1))
        cq8 = np.ascontiguousarray(np.concatenate([cq8f[acols.start:acols.stop],
                                                   cq8f[bcols.start:bcols.stop]], axis=0))
        sq8 = np.ascontiguousarray(np.concatenate([sq8f[acols.start:acols.stop],
                                                   sq8f[bcols.start:bcols.stop]], axis=0))
        in_maps.append({
            "xT": xT[b], "xq": xqc,
            "wqTs": wqTs, "wkT": wkTc, "wvT": wvTc, "woSW": woSW,
            "wgSW": wgSW, "wuSW": wuSW, "wdSW": wdSW,
            "sgT": sgT, "supT": supT, "sdT": sdT,
            "ck8": ck8, "sk8": sk8, "cq8": cq8, "sq8": sq8,
            "maskI": maskI, "ident": ident,
        })
    return in_maps


def kernel(**inputs):
    in_maps = _host_prep(**inputs)
    nc = _get_nc()
    res = run_bass_kernel_spmd(nc, in_maps, core_ids=list(range(8)))
    out = np.empty((2, S, D), dtype=np.float32)
    for c in range(8):
        b, j = c // 4, c % 4
        of = np.asarray(res.results[c]["out_fm"])
        out[b, 512 * j:512 * j + 512] = of[:, 0:512].T
        out[b, 512 * (7 - j):512 * (7 - j) + 512] = of[:, 512:1024].T
    return out


# version 12
# speedup vs baseline: 1.2441x; 1.0835x over previous
import numpy as np
import ml_dtypes
from contextlib import ExitStack

import concourse.bass as bass
import concourse.mybir as mybir
from concourse.bass_utils import run_bass_kernel_spmd
from concourse.alu_op_type import AluOpType

F32 = mybir.dt.float32
BF16 = mybir.dt.bfloat16
I8 = mybir.dt.int8
I32 = mybir.dt.int32
AF = mybir.ActivationFunctionType
AX = mybir.AxisListType
ET = mybir.EngineType
MUL = AluOpType.mult
ADD = AluOpType.add
SUB = AluOpType.subtract

D = 2048
S = 4096
NCH = 16
QL = 1024
HD = 64
INTER = 8192
EPS = 1e-5
EXP_SHIFT = -4.0
DEBUG = False


def build_nc():
    nc = bass.Bass("TRN2", target_bir_lowering=False)

    xT = nc.dram_tensor("xT", [D, S], F32, kind="ExternalInput").ap()
    xq = nc.dram_tensor("xq", [D, QL], F32, kind="ExternalInput").ap()
    wqTs = nc.dram_tensor("wqTs", [4, D, 512], F32, kind="ExternalInput").ap()
    wkT = nc.dram_tensor("wkT", [D, 512], F32, kind="ExternalInput").ap()
    wvT = nc.dram_tensor("wvT", [D, 512], F32, kind="ExternalInput").ap()
    woSW = nc.dram_tensor("woSW", [16, 128, 16, 128], F32, kind="ExternalInput").ap()
    wgSW = nc.dram_tensor("wgSW", [64, 128, 16, 128], I8, kind="ExternalInput").ap()
    wuSW = nc.dram_tensor("wuSW", [64, 128, 16, 128], I8, kind="ExternalInput").ap()
    wdSW = nc.dram_tensor("wdSW", [16, 128, 64, 128], I8, kind="ExternalInput").ap()
    sgT = nc.dram_tensor("sgT", [128, 64], F32, kind="ExternalInput").ap()
    supT = nc.dram_tensor("supT", [128, 64], F32, kind="ExternalInput").ap()
    sdT = nc.dram_tensor("sdT", [128, 16], F32, kind="ExternalInput").ap()
    ck8 = nc.dram_tensor("ck8", [S, 256], F32, kind="ExternalInput").ap()
    sk8 = nc.dram_tensor("sk8", [S, 256], F32, kind="ExternalInput").ap()
    cq8 = nc.dram_tensor("cq8", [QL, 256], F32, kind="ExternalInput").ap()
    sq8 = nc.dram_tensor("sq8", [QL, 256], F32, kind="ExternalInput").ap()
    maskI = nc.dram_tensor("maskI", [4, 128, 1024], F32, kind="ExternalInput").ap()
    ident = nc.dram_tensor("ident", [128, 128], F32, kind="ExternalInput").ap()
    out_fm = nc.dram_tensor("out_fm", [D, QL], F32, kind="ExternalOutput").ap()
    if DEBUG:
        dQ = nc.dram_tensor("dQ", [128, 16 * 1024], F32, kind="ExternalOutput").ap()
        dK = nc.dram_tensor("dK", [128, 4 * 4096], F32, kind="ExternalOutput").ap()
        dV = nc.dram_tensor("dV", [128, 32 * 512], F32, kind="ExternalOutput").ap()
        dO = nc.dram_tensor("dO", [128, 16 * 1024], F32, kind="ExternalOutput").ap()
    y1sp = nc.dram_tensor("y1sp", [D, QL], F32, kind="Internal").ap()

    ctx = ExitStack()
    nc._kb_ctx = ctx
    SEM = {}
    CNT = {}
    for s_ in ("LG", "LS", "T", "SC", "V", "STO"):
        SEM[s_] = ctx.enter_context(nc.semaphore(s_))
        CNT[s_] = 0

    def INC(instr, s_, k=1):
        instr.then_inc(SEM[s_], k)
        CNT[s_] += k
        return CNT[s_]

    E = ctx.enter_context
    id_bf = E(nc.sbuf_tensor("id_bf", [128, 128], BF16))
    ones1b = E(nc.sbuf_tensor("ones1b", [128, 1], BF16))
    ones1f = E(nc.sbuf_tensor("ones1f", [1, 128], F32))
    sg_sb = E(nc.sbuf_tensor("sg_sb", [128, 64], F32))
    sup_sb = E(nc.sbuf_tensor("sup_sb", [128, 64], F32))
    sd_sb = E(nc.sbuf_tensor("sd_sb", [128, 16], F32))
    consts = E(nc.sbuf_tensor("consts", [128, 2], F32))
    fence = E(nc.sbuf_tensor("fence", [128, 8], F32))
    O_fm = E(nc.sbuf_tensor("O_fm", [128, 16 * 1024], BF16))

    INC(nc.sync.dma_start(out=sg_sb[:], in_=sgT), "LS", 16)
    INC(nc.sync.dma_start(out=sup_sb[:], in_=supT), "LS", 16)
    INC(nc.sync.dma_start(out=sd_sb[:], in_=sdT), "LS", 16)
    INC(nc.gpsimd.dma_start(out=id_bf[:], in_=ident), "LG", 16)
    nc.vector.memset(consts[:, 0:1], EXP_SHIFT)
    nc.vector.memset(consts[:, 1:2], EPS)
    nc.vector.memset(ones1b[:], 1.0)
    INC(nc.vector.memset(ones1f[:], 1.0), "V", 1)

    jval = nc.partition_id([ET.PE, ET.Activation, ET.DVE]) % 4

    def rope6(psrc, dst, ct, st, scrA):
        g3 = lambda ap: ap.rearrange("p (g c two) -> p g c two", g=8, two=2)
        r_ = g3(psrc)[:, :, :, 0]
        i_ = g3(psrc)[:, :, :, 1]
        c_ = ct.rearrange("p (g c) -> p g c", g=8)
        s_ = st.rearrange("p (g c) -> p g c", g=8)
        t1 = scrA.ap()[:, 0:256].rearrange("p (g c) -> p g c", g=8)
        t2 = scrA.ap()[:, 256:512].rearrange("p (g c) -> p g c", g=8)
        od = g3(dst)
        nc.vector.tensor_tensor(t1, r_, c_, MUL)
        nc.vector.tensor_tensor(t2, i_, s_, MUL)
        nc.vector.tensor_tensor(od[:, :, :, 0], t1, t2, SUB)
        nc.vector.tensor_tensor(t1, r_, s_, MUL)
        nc.vector.tensor_tensor(t2, i_, c_, MUL)
        return nc.vector.tensor_tensor(od[:, :, :, 1], t1, t2, ADD)

    with ExitStack() as s1:
        E1 = s1.enter_context
        scrA = E1(nc.sbuf_tensor("scrA", [128, 512], F32))
        Q_all = E1(nc.sbuf_tensor("Q_all", [128, 16 * 1024], BF16))
        K_all = E1(nc.sbuf_tensor("K_all", [128, 4 * 4096], BF16))
        tabc = [E1(nc.sbuf_tensor("tabc0", [128, 256], F32))] * 2
        tabs = [E1(nc.sbuf_tensor("tabs0", [128, 256], F32))] * 2
        stag = [E1(nc.sbuf_tensor(f"stag{i}", [128, 512], BF16)) for i in range(2)]

        with ExitStack() as s3, \
             nc.psum_tensor("psQ0", [128, 512], F32) as psQ0, \
             nc.psum_tensor("psQ1", [128, 512], F32) as psQ1, \
             nc.psum_tensor("psT", [128, 1024], BF16) as psT:
            E3 = s3.enter_context
            xq_bf = E3(nc.sbuf_tensor("xq_bf", [128, 16 * 1024], BF16))
            wq_b = [E3(nc.sbuf_tensor(f"wq_sb{i}", [128, 16 * 512], BF16)) for i in range(2)]
            psQ = [psQ0, psQ1]

            xq3 = xq.rearrange("(c p) t -> p c t", p=128)
            xqb3 = xq_bf.ap().rearrange("p (c t) -> p c t", t=1024)
            for hh in range(4):
                INC(nc.gpsimd.dma_start(out=xqb3[:, 4 * hh:4 * hh + 4, :],
                                        in_=xq3[:, 4 * hh:4 * hh + 4, :]), "LG", 16)
            xq_ready = CNT["LG"]

            rope_done = {}
            tpose_done = {}
            evac_done = {}
            mm_done = {}
            tab_done = {}
            wq_done = {}

            def load_wq(s_):
                if s_ >= 2:
                    nc.gpsimd.wait_ge(SEM["T"], mm_done[s_ * 8 - 9])
                srcw = wqTs[s_].rearrange("(c p) n -> p c n", p=128)
                w3 = wq_b[s_ % 2].ap().rearrange("p (c n) -> p c n", n=512)
                for hh in range(2):
                    INC(nc.gpsimd.dma_start(out=w3[:, 8 * hh:8 * hh + 8, :],
                                            in_=srcw[:, 8 * hh:8 * hh + 8, :]), "LG", 16)
                wq_done[s_] = CNT["LG"]

            load_wq(0)
            load_wq(1)

            def q_transpose(u):
                s_, tc = u // 8, u % 8
                par = u % 2
                nc.tensor.wait_ge(SEM["V"], rope_done[u])
                if u >= 2:
                    nc.tensor.wait_ge(SEM["V"], evac_done[u - 2])
                off = par * 512
                for p in range(4):
                    tp = nc.tensor.transpose(psT[:, off + p * 128: off + (p + 1) * 128],
                                             stag[par][:, p * 128:(p + 1) * 128], id_bf[:])
                tpose_done[u] = INC(tp, "T")
                nc.vector.wait_ge(SEM["T"], tpose_done[u])
                dst = Q_all.ap().rearrange("p (g t) -> p g t", t=1024)[:, 4 * s_:4 * s_ + 4, tc * 128:(tc + 1) * 128]
                src_ = psT.ap()[:, off:off + 512].rearrange("p (g t) -> p g t", t=128)
                evac_done[u] = INC(nc.vector.tensor_copy(dst, src_), "V")

            for s_ in range(4):
                for tc in range(8):
                    u = s_ * 8 + tc
                    par = u % 2
                    if u >= 1:
                        nc.sync.wait_ge(SEM["V"], rope_done[u - 1])
                    INC(nc.sync.dma_start(out=tabc[par][:], in_=cq8[tc * 128:(tc + 1) * 128, :]), "LS", 16)
                    INC(nc.sync.dma_start(out=tabs[par][:], in_=sq8[tc * 128:(tc + 1) * 128, :]), "LS", 16)
                    tab_done[u] = CNT["LS"]

                    nc.tensor.wait_ge(SEM["LG"], max(xq_ready, wq_done[s_]))
                    if u >= 2:
                        nc.tensor.wait_ge(SEM["V"], rope_done[u - 2])
                    for c in range(NCH):
                        mm = nc.tensor.matmul(psQ[par][:], xq_bf[:, c * 1024 + tc * 128: c * 1024 + (tc + 1) * 128],
                                              wq_b[s_ % 2][:, c * 512:(c + 1) * 512],
                                              start=(c == 0), stop=(c == NCH - 1))
                    mm_done[u] = INC(mm, "T")

                    if u >= 1:
                        q_transpose(u - 1)

                    nc.vector.wait_ge(SEM["T"], mm_done[u])
                    nc.vector.wait_ge(SEM["LS"], tab_done[u])
                    if u >= 2:
                        nc.vector.wait_ge(SEM["T"], tpose_done[u - 2])
                    rope_done[u] = INC(rope6(psQ[par][:], stag[par][:], tabc[par][:], tabs[par][:], scrA), "V")
                if s_ + 2 < 4:
                    load_wq(s_ + 2)
            q_transpose(31)
            p1b_end_V = CNT["V"]
        V_ext = E1(nc.sbuf_tensor("V_ext", [128, 32 * 1024 + 64], BF16))
        with ExitStack() as s2, \
             nc.psum_tensor("psK0", [128, 512], F32) as psK0, \
             nc.psum_tensor("psK1", [128, 512], F32) as psK1, \
             nc.psum_tensor("psV0", [128, 512], F32) as psV0, \
             nc.psum_tensor("psV1", [128, 512], F32) as psV1, \
             nc.psum_tensor("psTa", [128, 1024], BF16) as psTa:
            E2 = s2.enter_context
            xblk = [E2(nc.sbuf_tensor(f"xblk{i}", [128, 16 * 128], BF16)) for i in range(2)]
            wkv_sb = E2(nc.sbuf_tensor("wkv_sb", [128, 16 * 1024], BF16))
            psK = [psK0, psK1]
            psV = [psV0, psV1]

            vx = V_ext.ap()
            ones_pat = vx[:, 0:32 * 1024].rearrange("p (t k b) -> p t k b", k=8, b=128)[:, :, :, 0:64]
            nc.vector.memset(ones_pat, 1.0)
            vext_ones_V = INC(nc.vector.memset(vx[:, 32 * 1024:32 * 1024 + 64], 1.0), "V")

            xT3 = xT.rearrange("(c p) t -> p c t", p=128)
            xb_done = {}

            def load_xblk(tc, wait_t):
                if wait_t is not None:
                    nc.gpsimd.wait_ge(SEM["T"], wait_t)
                d3 = xblk[tc % 2].ap().rearrange("p (c t) -> p c t", t=128)
                INC(nc.gpsimd.dma_start(out=d3, in_=xT3[:, :, tc * 128:(tc + 1) * 128]), "LG", 16)
                xb_done[tc] = CNT["LG"]

            nc.gpsimd.wait_ge(SEM["V"], p1b_end_V)
            w4 = wkv_sb.ap().rearrange("p (c two n) -> p c two n", two=2, n=512)
            srck = wkT.rearrange("(c p) n -> p c n", p=128)
            srcv = wvT.rearrange("(c p) n -> p c n", p=128)
            INC(nc.gpsimd.dma_start(out=w4[:, :, 0, :], in_=srck), "LG", 16)
            INC(nc.gpsimd.dma_start(out=w4[:, :, 1, :], in_=srcv), "LG", 16)
            wk_ready = CNT["LG"]

            kmm_done = {}
            vmm_done = {}
            ropek_done = {}
            vevac_done = {}
            ktp_done = {}
            kevac_done = {}
            tabk_done = {}

            def k_transpose(tc):
                par = tc % 2
                nc.tensor.wait_ge(SEM["V"], ropek_done[tc])
                if tc >= 2:
                    nc.tensor.wait_ge(SEM["V"], kevac_done[tc - 2])
                off = par * 512
                for p in range(4):
                    tp = nc.tensor.transpose(psTa[:, off + p * 128: off + (p + 1) * 128],
                                             stag[par][:, p * 128:(p + 1) * 128], id_bf[:])
                ktp_done[tc] = INC(tp, "T")
                nc.vector.wait_ge(SEM["T"], ktp_done[tc])
                dst = K_all.ap().rearrange("p (b t) -> p b t", t=4096)[:, :, tc * 128:(tc + 1) * 128]
                src_ = psTa.ap()[:, off:off + 512].rearrange("p (b t) -> p b t", t=128)
                kevac_done[tc] = INC(nc.vector.tensor_copy(dst, src_), "V")

            load_xblk(0, None)
            load_xblk(1, None)
            for tc in range(32):
                par = tc % 2
                if tc < 2:
                    nc.sync.wait_ge(SEM["V"], p1b_end_V)
                if tc >= 1:
                    nc.sync.wait_ge(SEM["V"], ropek_done[tc - 1])
                INC(nc.sync.dma_start(out=tabc[par][:], in_=ck8[tc * 128:(tc + 1) * 128, :]), "LS", 16)
                INC(nc.sync.dma_start(out=tabs[par][:], in_=sk8[tc * 128:(tc + 1) * 128, :]), "LS", 16)
                tabk_done[tc] = CNT["LS"]

                nc.tensor.wait_ge(SEM["LG"], max(wk_ready, xb_done[tc]))
                if tc < 2:
                    nc.tensor.wait_ge(SEM["V"], max(p1b_end_V, vext_ones_V))
                if tc >= 2:
                    nc.tensor.wait_ge(SEM["V"], ropek_done[tc - 2])
                    nc.tensor.wait_ge(SEM["V"], vevac_done[tc - 2])
                for c in range(NCH):
                    mm = nc.tensor.matmul(psK[par][:], xblk[par][:, c * 128:(c + 1) * 128],
                                          wkv_sb[:, c * 1024:c * 1024 + 512],
                                          start=(c == 0), stop=(c == NCH - 1))
                kmm_done[tc] = INC(mm, "T")
                for c in range(NCH):
                    mm = nc.tensor.matmul(psV[par][:], xblk[par][:, c * 128:(c + 1) * 128],
                                          wkv_sb[:, c * 1024 + 512:(c + 1) * 1024],
                                          start=(c == 0), stop=(c == NCH - 1))
                vmm_done[tc] = INC(mm, "T")

                if tc >= 1:
                    k_transpose(tc - 1)

                nc.vector.wait_ge(SEM["T"], kmm_done[tc])
                nc.vector.wait_ge(SEM["LS"], tabk_done[tc])
                if tc >= 2:
                    nc.vector.wait_ge(SEM["T"], ktp_done[tc - 2])
                ropek_done[tc] = INC(rope6(psK[par][:], stag[par][:], tabc[par][:], tabs[par][:], scrA), "V")

                nc.vector.wait_ge(SEM["T"], vmm_done[tc])
                dstv = vx[:, tc * 1024 + 64:(tc + 1) * 1024 + 64].rearrange(
                    "p (k b) -> p k b", b=128)[:, :, 0:64]
                vevac_done[tc] = INC(nc.vector.tensor_copy(dstv, psV[par][:]), "V")

                if tc + 2 < 32:
                    load_xblk(tc + 2, vmm_done[tc])
            k_transpose(31)
            p1a_end = dict(CNT)

        with ExitStack() as s4, \
             nc.psum_tensor("psS0", [128, 1024], F32) as psS0, \
             nc.psum_tensor("psS1", [128, 1024], F32) as psS1, \
             nc.psum_tensor("psOlo0", [128, 512], F32) as psOlo0, \
             nc.psum_tensor("psOlo1", [128, 512], F32) as psOlo1, \
             nc.psum_tensor("psOhi0", [128, 512], F32) as psOhi0, \
             nc.psum_tensor("psOhi1", [128, 512], F32) as psOhi1:
            E4 = s4.enter_context
            PT = [E4(nc.sbuf_tensor(f"PT{i}", [128, 1024], BF16)) for i in range(2)]
            linv = E4(nc.sbuf_tensor("linv", [128, 1024], F32))
            lcp = E4(nc.sbuf_tensor("lcp", [128, 1024], F32))
            mask_bf = E4(nc.sbuf_tensor("mask_bf", [128, 4 * 1024], BF16))
            psS = [psS0, psS1]
            psOlo = [psOlo0, psOlo1]
            psOhi = [psOhi0, psOhi1]

            nc.gpsimd.wait_ge(SEM["V"], p1a_end["V"])
            INC(nc.gpsimd.dma_start(out=mask_bf.ap().rearrange("p (s q) -> p s q", q=1024),
                                    in_=maskI.rearrange("s p q -> p s q")), "LG", 16)
            mask_ready = CNT["LG"]

            base_cnt = dict(CNT)

            def emit_attn(j, emit):
                cnt = dict(base_cnt)

                def inc(instr, s_, k=1):
                    if emit:
                        instr.then_inc(SEM[s_], k)
                    cnt[s_] += k
                    return cnt[s_]

                def w(eng, s_, v):
                    if emit:
                        eng.wait_ge(SEM[s_], v)

                KT_A = 4 * (j + 1)
                KT_B = 4 * (8 - j)
                exp_done = {}
                pv_done = {}
                evac_slot = [None, None]
                gu = 0
                hc2 = 0
                w(nc.tensor, "LG", mask_ready)
                w(nc.vector, "LG", mask_ready)
                w(nc.scalar, "V", base_cnt["V"])
                pending_evac = []

                def emit_evac(slot, hq, last_pv):
                    w(nc.vector, "T", last_pv)
                    hcs = slice(slot * 512, slot * 512 + 512)
                    tt2 = None
                    if emit:
                        nc.vector.tensor_copy(lcp[64:128, hcs], psOlo[slot][64:128, :])
                        nc.vector.tensor_copy(lcp[0:64, hcs], psOhi[slot][0:64, :])
                        nc.vector.reciprocal(linv[:, hcs], lcp[:, hcs])
                        nc.vector.tensor_tensor(O_fm[0:64, hq:hq + 512],
                                                psOlo[slot][0:64, :], linv[64:128, hcs], MUL)
                        tt2 = nc.vector.tensor_tensor(O_fm[64:128, hq:hq + 512],
                                                      psOhi[slot][64:128, :], linv[0:64, hcs], MUL)
                    evac_slot[slot] = inc(tt2, "V")

                for g in range(16):
                    blkp = g // 4
                    kvh_lo, kvh_hi = 2 * blkp, 2 * blkp + 1
                    qcol = g * 1024
                    for half in (1, 0):
                        KT = KT_A if half == 0 else KT_B
                        hq = qcol + half * 512
                        u0 = gu
                        for u in range(KT + 1):
                            if u < KT:
                                uu = u0 + u
                                par = uu % 2
                                if uu >= 2:
                                    w(nc.tensor, "SC", exp_done[uu - 2])
                                kc = blkp * 4096 + u * 128
                                diag = KT - 4 <= u
                                s_idx = u - (KT - 4)
                                if emit:
                                    nc.tensor.matmul(psS[par][:, 0:512], K_all[0:64, kc:kc + 128],
                                                     Q_all[0:64, hq:hq + 512], start=True, stop=not diag)
                                    mm = nc.tensor.matmul(psS[par][:, 512:1024], K_all[64:128, kc:kc + 128],
                                                          Q_all[64:128, hq:hq + 512], start=True, stop=not diag)
                                    if diag:
                                        nc.tensor.matmul(psS[par][:, 0:512], id_bf[:],
                                                         mask_bf[:, s_idx * 1024:s_idx * 1024 + 512],
                                                         start=False, stop=True)
                                        mm = nc.tensor.matmul(psS[par][:, 512:1024], id_bf[:],
                                                              mask_bf[:, s_idx * 1024:s_idx * 1024 + 512],
                                                              start=False, stop=True)
                                else:
                                    mm = None
                                smm = inc(mm, "T")

                                w(nc.scalar, "T", smm)
                                if uu >= 2:
                                    w(nc.scalar, "T", pv_done[uu - 2])
                                act = None
                                if emit:
                                    act = nc.scalar.activation(PT[par][:], psS[par][:], AF.Exp,
                                                               bias=consts[:, 0:1])
                                exp_done[uu] = inc(act, "SC")

                            if u >= 1:
                                v_ = u - 1
                                uu = u0 + v_
                                par = uu % 2
                                w(nc.tensor, "SC", exp_done[uu])
                                slot = hc2 % 2
                                if v_ == 0 and evac_slot[slot] is not None:
                                    w(nc.tensor, "V", evac_slot[slot])
                                win_lo = v_ * 1024 + kvh_lo * 128 + 64
                                win_hi = v_ * 1024 + kvh_hi * 128
                                mm = None
                                if emit:
                                    nc.tensor.matmul(psOlo[slot][:], V_ext[:, win_lo:win_lo + 128],
                                                     PT[par][:, 0:512],
                                                     start=(v_ == 0), stop=(v_ == KT - 1))
                                    mm = nc.tensor.matmul(psOhi[slot][:], V_ext[:, win_hi:win_hi + 128],
                                                          PT[par][:, 512:1024],
                                                          start=(v_ == 0), stop=(v_ == KT - 1))
                                pv_done[uu] = inc(mm, "T")
                        gu += KT

                        pending_evac.append((hc2 % 2, hq, pv_done[gu - 1]))
                        hc2 += 1
                        if len(pending_evac) > 1:
                            emit_evac(*pending_evac.pop(0))
                for ev in pending_evac:
                    emit_evac(*ev)
                return cnt

            deltas = []
            for j in range(4):
                c_ = emit_attn(j, emit=False)
                deltas.append({k: c_[k] - base_cnt[k] for k in c_})
            maxd = {k: max(d[k] for d in deltas) for k in deltas[0]}

            for body in nc.Switch(engines=[nc.tensor, nc.scalar, nc.vector],
                                  index=[jval, jval, jval], n=4):
                endc = emit_attn(body, emit=True)
                for (eng, s_) in ((nc.tensor, "T"), (nc.scalar, "SC"), (nc.vector, "V")):
                    pad = (base_cnt[s_] + maxd[s_]) - endc[s_]
                    if pad > 0:
                        eng.sem_inc(SEM[s_], pad)
            for k in maxd:
                CNT[k] = base_cnt[k] + maxd[k]
            p2_end = dict(CNT)

    with ExitStack() as sh:
        Eh = sh.enter_context
        bounce = [Eh(nc.sbuf_tensor(f"bounce{i}", [128, 512], F32)) for i in range(2)]
        scrB = Eh(nc.sbuf_tensor("scrB", [128, 1024], F32))
        rows = Eh(nc.sbuf_tensor("rows", [1, 2048], F32))
        mB = Eh(nc.sbuf_tensor("mB", [128, 512], F32))
        rB = Eh(nc.sbuf_tensor("rB", [128, 512], F32))
        y1bfh = Eh(nc.sbuf_tensor("y1bfh", [128, 16 * 512], BF16))

        prev_V = p2_end["V"]
        prev_STO = CNT["STO"]
        for half in range(2):
            hc = half * 512
            with ExitStack() as s5, \
                 nc.psum_tensor(f"psAO0_{half}", [128, 512], F32) as psAO0, \
                 nc.psum_tensor(f"psAO1_{half}", [128, 512], F32) as psAO1, \
                 nc.psum_tensor(f"psSt1_{half}", [1, 512], F32) as psSt1, \
                 nc.psum_tensor(f"psSt2_{half}", [1, 512], F32) as psSt2, \
                 nc.psum_tensor(f"psB_{half}", [128, 1024], F32) as psB:
                E5 = s5.enter_context
                y1pre = E5(nc.sbuf_tensor(f"y1pre_{half}", [128, 16 * 512], F32))
                woT_b = [E5(nc.sbuf_tensor(f"woT{i}_{half}", [128, 2048], BF16)) for i in range(2)]
                xqf = [E5(nc.sbuf_tensor(f"xqf{i}_{half}", [128, 512], F32)) for i in range(2)]
                sqs = [E5(nc.sbuf_tensor(f"sqs{i}_{half}", [128, 512], BF16)) for i in range(2)]
                psAO = [psAO0, psAO1]

                ao_done = {}
                pre_done = {}
                st_done = {}
                wo_done = {}
                xq_done = {}
                for m in range(16):
                    par = m % 2
                    if m < 2:
                        nc.gpsimd.wait_ge(SEM["V"], prev_V)
                        nc.sync.wait_ge(SEM["V"], prev_V)
                    if m >= 2:
                        nc.gpsimd.wait_ge(SEM["T"], ao_done[m - 2])
                        nc.sync.wait_ge(SEM["V"], pre_done[m - 2])
                    INC(nc.gpsimd.dma_start(out=woT_b[par].ap().rearrange("p (c j) -> p c j", j=128),
                                            in_=woSW[m]), "LG", 16)
                    wo_done[m] = CNT["LG"]
                    INC(nc.sync.dma_start(out=xqf[par][:], in_=xq[m * 128:(m + 1) * 128, hc:hc + 512]), "LS", 16)
                    xq_done[m] = CNT["LS"]

                    nc.tensor.wait_ge(SEM["LG"], wo_done[m])
                    if m == 0:
                        nc.tensor.wait_ge(SEM["V"], prev_V)
                    if m >= 2:
                        nc.tensor.wait_ge(SEM["V"], pre_done[m - 2])
                    for c in range(16):
                        mm = nc.tensor.matmul(psAO[par][:], woT_b[par][:, c * 128:(c + 1) * 128],
                                              O_fm[:, c * 1024 + hc: c * 1024 + hc + 512],
                                              start=(c == 0), stop=(c == 15))
                    ao_done[m] = INC(mm, "T")

                    nc.vector.wait_ge(SEM["T"], ao_done[m])
                    nc.vector.wait_ge(SEM["LS"], xq_done[m])
                    if m >= 2:
                        nc.vector.wait_ge(SEM["T"], st_done[m - 2])
                    nc.vector.tensor_tensor(y1pre[:, m * 512:(m + 1) * 512], psAO[par][:], xqf[par][:], ADD)
                    nc.vector.tensor_copy(y1bfh[:, m * 512:(m + 1) * 512], y1pre[:, m * 512:(m + 1) * 512])
                    pre_done[m] = INC(nc.vector.tensor_tensor(sqs[par][:], y1bfh[:, m * 512:(m + 1) * 512],
                                                              y1bfh[:, m * 512:(m + 1) * 512], MUL), "V")

                    nc.tensor.wait_ge(SEM["V"], pre_done[m])
                    nc.tensor.matmul(psSt1[:], ones1b[:], y1bfh[:, m * 512:(m + 1) * 512],
                                     start=(m == 0), stop=(m == 15))
                    mm = nc.tensor.matmul(psSt2[:], ones1b[:], sqs[par][:],
                                          start=(m == 0), stop=(m == 15))
                    st_done[m] = INC(mm, "T")

                nc.vector.wait_ge(SEM["T"], st_done[15])
                nc.vector.tensor_scalar_mul(rows[0:1, 0:512], psSt1[:], 1.0 / D)
                nc.vector.tensor_scalar_mul(rows[0:1, 512:1024], psSt2[:], 1.0 / D)
                nc.vector.tensor_tensor(rows[0:1, 1024:1536], rows[0:1, 0:512], rows[0:1, 0:512], MUL)
                vfin = INC(nc.vector.tensor_tensor(rows[0:1, 1536:2048], rows[0:1, 512:1024],
                                                   rows[0:1, 1024:1536], SUB), "V")
                nc.scalar.wait_ge(SEM["V"], vfin)
                nc.scalar.activation(rows[0:1, 1024:1536], rows[0:1, 1536:2048], AF.Ln, bias=consts[0:1, 1:2])
                scf = INC(nc.scalar.activation(rows[0:1, 512:1024], rows[0:1, 1024:1536], AF.Exp, scale=-0.5), "SC")
                nc.tensor.wait_ge(SEM["SC"], scf)
                nc.tensor.matmul(psB[:, 0:512], ones1f[:], rows[0:1, 0:512], start=True, stop=True)
                bts = INC(nc.tensor.matmul(psB[:, 512:1024], ones1f[:], rows[0:1, 512:1024],
                                           start=True, stop=True), "T")
                nc.vector.wait_ge(SEM["T"], bts)
                nc.vector.tensor_copy(mB[:], psB[:, 0:512])
                bev = INC(nc.vector.tensor_copy(rB[:], psB[:, 512:1024]), "V")

                norm_done = {}
                spill_done = {}
                cast_done = {}
                for m in range(16):
                    par = m % 2
                    nc.vector.wait_ge(SEM["T"], bts)
                    if m < 2:
                        nc.vector.wait_ge(SEM["STO"], prev_STO)
                    if m >= 2:
                        nc.vector.wait_ge(SEM["STO"], spill_done[m - 2])
                    if m >= 2:
                        nc.vector.wait_ge(SEM["SC"], cast_done[m - 2])
                    nc.vector.tensor_tensor(scrB[:, 0:512], y1pre[:, m * 512:(m + 1) * 512], mB[:], SUB)
                    norm_done[m] = INC(nc.vector.tensor_tensor(bounce[par][:], scrB[:, 0:512], rB[:], MUL), "V")
                    nc.scalar.wait_ge(SEM["V"], norm_done[m])
                    cast_done[m] = INC(nc.scalar.copy(y1bfh[:, m * 512:(m + 1) * 512], bounce[par][:]), "SC")
                    nc.sync.wait_ge(SEM["V"], norm_done[m])
                    spill_done[m] = INC(nc.sync.dma_start(out=y1sp[m * 128:(m + 1) * 128, hc:hc + 512],
                                                          in_=bounce[par][:]), "STO", 16)
                p3_end = dict(CNT)

            with ExitStack() as s6:
                E6 = s6.enter_context
                f_sb = E6(nc.sbuf_tensor(f"f_sb_{half}", [128, 64 * 512], BF16))
                with ExitStack() as s6a, \
                     nc.psum_tensor(f"psG0_{half}", [128, 512], F32) as psG0, \
                     nc.psum_tensor(f"psG1_{half}", [128, 512], F32) as psG1, \
                     nc.psum_tensor(f"psU0_{half}", [128, 512], F32) as psU0, \
                     nc.psum_tensor(f"psU1_{half}", [128, 512], F32) as psU1:
                    E6a = s6a.enter_context
                    wg_b = [E6a(nc.sbuf_tensor(f"wg{i}_{half}", [128, 2048], BF16)) for i in range(2)]
                    wu_b = [E6a(nc.sbuf_tensor(f"wu{i}_{half}", [128, 2048], BF16)) for i in range(2)]
                    gsc = [E6a(nc.sbuf_tensor(f"gsc{i}_{half}", [128, 512], BF16)) for i in range(2)]
                    usc = [E6a(nc.sbuf_tensor(f"usc{i}_{half}", [128, 512], BF16)) for i in range(2)]
                    psG = [psG0, psG1]
                    psU = [psU0, psU1]

                    gu_done = {}
                    silu_done = {}
                    fdone = {}
                    wld = {}
                    for ic in range(64):
                        par = ic % 2
                        if ic < 2:
                            nc.gpsimd.wait_ge(SEM["V"], p3_end["V"])
                        if ic >= 2:
                            nc.gpsimd.wait_ge(SEM["T"], gu_done[ic - 2])
                        INC(nc.gpsimd.dma_start(out=wg_b[par].ap().rearrange("p (c j) -> p c j", j=128),
                                                in_=wgSW[ic]), "LG", 16)
                        INC(nc.gpsimd.dma_start(out=wu_b[par].ap().rearrange("p (c j) -> p c j", j=128),
                                                in_=wuSW[ic]), "LG", 16)
                        wld[ic] = CNT["LG"]

                        nc.tensor.wait_ge(SEM["LG"], wld[ic])
                        if ic == 0:
                            nc.tensor.wait_ge(SEM["V"], p3_end["V"])
                            nc.tensor.wait_ge(SEM["SC"], p3_end["SC"])
                        if ic >= 2:
                            nc.tensor.wait_ge(SEM["SC"], silu_done[ic - 2])
                            nc.tensor.wait_ge(SEM["V"], fdone[ic - 2])
                        for c in range(16):
                            nc.tensor.matmul(psG[par][:], wg_b[par][:, c * 128:(c + 1) * 128],
                                             y1bfh[:, c * 512:(c + 1) * 512],
                                             start=(c == 0), stop=(c == 15))
                        for c in range(16):
                            mm = nc.tensor.matmul(psU[par][:], wu_b[par][:, c * 128:(c + 1) * 128],
                                                  y1bfh[:, c * 512:(c + 1) * 512],
                                                  start=(c == 0), stop=(c == 15))
                        gu_done[ic] = INC(mm, "T")

                        nc.scalar.wait_ge(SEM["T"], gu_done[ic])
                        if ic >= 2:
                            nc.scalar.wait_ge(SEM["V"], fdone[ic - 2])
                        silu_done[ic] = INC(nc.scalar.activation(gsc[par][:], psG[par][:], AF.Silu,
                                                                 scale=sg_sb[:, ic:ic + 1]), "SC")
                        nc.vector.wait_ge(SEM["T"], gu_done[ic])
                        nc.vector.tensor_scalar_mul(usc[par][:], psU[par][:], sup_sb[:, ic:ic + 1])
                        nc.vector.wait_ge(SEM["SC"], silu_done[ic])
                        fdone[ic] = INC(nc.vector.tensor_tensor(f_sb[:, ic * 512:(ic + 1) * 512],
                                                                gsc[par][:], usc[par][:], MUL), "V")
                    p4a_end = dict(CNT)

                with ExitStack() as s6b, \
                     nc.psum_tensor(f"psD0_{half}", [128, 512], F32) as psD0, \
                     nc.psum_tensor(f"psD1_{half}", [128, 512], F32) as psD1, \
                     nc.psum_tensor(f"psSt3_{half}", [1, 512], F32) as psSt3, \
                     nc.psum_tensor(f"psSt4_{half}", [1, 512], F32) as psSt4, \
                     nc.psum_tensor(f"psB2_{half}", [128, 1024], F32) as psB2:
                    E6b = s6b.enter_context
                    wd_b = [E6b(nc.sbuf_tensor(f"wd{i}_{half}", [128, 8192], BF16)) for i in range(2)]
                    y2pre = E6b(nc.sbuf_tensor(f"y2pre_{half}", [128, 16 * 512], F32))
                    y1re = [E6b(nc.sbuf_tensor(f"y1re{i}_{half}", [128, 512], F32)) for i in range(2)]
                    y2bf = [E6b(nc.sbuf_tensor(f"y2bf{i}_{half}", [128, 512], BF16)) for i in range(2)]
                    sq2 = [E6b(nc.sbuf_tensor(f"sq2{i}_{half}", [128, 512], BF16)) for i in range(2)]
                    psD = [psD0, psD1]

                    dmm_done = {}
                    d_ev = {}
                    st2_done = {}
                    wdld = {}
                    y1ld = {}
                    for m2 in range(16):
                        par = m2 % 2
                        if m2 < 2:
                            nc.gpsimd.wait_ge(SEM["V"], p4a_end["V"])
                        if m2 >= 2:
                            nc.gpsimd.wait_ge(SEM["T"], dmm_done[m2 - 2])
                            nc.sync.wait_ge(SEM["V"], d_ev[m2 - 2])
                        INC(nc.gpsimd.dma_start(out=wd_b[par].ap().rearrange("p (c j) -> p c j", j=128),
                                                in_=wdSW[m2]), "LG", 16)
                        wdld[m2] = CNT["LG"]
                        nc.sync.wait_ge(SEM["STO"], p3_end["STO"])
                        INC(nc.sync.dma_start(out=y1re[par][:], in_=y1sp[m2 * 128:(m2 + 1) * 128, hc:hc + 512]),
                            "LS", 16)
                        y1ld[m2] = CNT["LS"]

                        nc.tensor.wait_ge(SEM["LG"], wdld[m2])
                        if m2 == 0:
                            nc.tensor.wait_ge(SEM["V"], p4a_end["V"])
                        if m2 >= 2:
                            nc.tensor.wait_ge(SEM["V"], d_ev[m2 - 2])
                        for ic in range(64):
                            mm = nc.tensor.matmul(psD[par][:], wd_b[par][:, ic * 128:(ic + 1) * 128],
                                                  f_sb[:, ic * 512:(ic + 1) * 512],
                                                  start=(ic == 0), stop=(ic == 63))
                        dmm_done[m2] = INC(mm, "T")

                        nc.vector.wait_ge(SEM["T"], dmm_done[m2])
                        nc.vector.wait_ge(SEM["LS"], y1ld[m2])
                        if m2 >= 2:
                            nc.vector.wait_ge(SEM["T"], st2_done[m2 - 2])
                        nc.vector.tensor_scalar_mul(scrB[:, 0:512], psD[par][:], sd_sb[:, m2:m2 + 1])
                        nc.vector.tensor_tensor(y2pre[:, m2 * 512:(m2 + 1) * 512], scrB[:, 0:512],
                                                y1re[par][:], ADD)
                        nc.vector.tensor_copy(y2bf[par][:], y2pre[:, m2 * 512:(m2 + 1) * 512])
                        d_ev[m2] = INC(nc.vector.tensor_tensor(sq2[par][:], y2bf[par][:],
                                                               y2bf[par][:], MUL), "V")

                        nc.tensor.wait_ge(SEM["V"], d_ev[m2])
                        nc.tensor.matmul(psSt3[:], ones1b[:], y2bf[par][:], start=(m2 == 0), stop=(m2 == 15))
                        mm = nc.tensor.matmul(psSt4[:], ones1b[:], sq2[par][:], start=(m2 == 0), stop=(m2 == 15))
                        st2_done[m2] = INC(mm, "T")

                    nc.vector.wait_ge(SEM["T"], st2_done[15])
                    nc.vector.tensor_scalar_mul(rows[0:1, 0:512], psSt3[:], 1.0 / D)
                    nc.vector.tensor_scalar_mul(rows[0:1, 512:1024], psSt4[:], 1.0 / D)
                    nc.vector.tensor_tensor(rows[0:1, 1024:1536], rows[0:1, 0:512], rows[0:1, 0:512], MUL)
                    vfin = INC(nc.vector.tensor_tensor(rows[0:1, 1536:2048], rows[0:1, 512:1024],
                                                       rows[0:1, 1024:1536], SUB), "V")
                    nc.scalar.wait_ge(SEM["V"], vfin)
                    nc.scalar.activation(rows[0:1, 1024:1536], rows[0:1, 1536:2048], AF.Ln, bias=consts[0:1, 1:2])
                    scf = INC(nc.scalar.activation(rows[0:1, 512:1024], rows[0:1, 1024:1536],
                                                   AF.Exp, scale=-0.5), "SC")
                    nc.tensor.wait_ge(SEM["SC"], scf)
                    nc.tensor.matmul(psB2[:, 0:512], ones1f[:], rows[0:1, 0:512], start=True, stop=True)
                    bts = INC(nc.tensor.matmul(psB2[:, 512:1024], ones1f[:], rows[0:1, 512:1024],
                                               start=True, stop=True), "T")
                    nc.vector.wait_ge(SEM["T"], bts)
                    nc.vector.tensor_copy(mB[:], psB2[:, 0:512])
                    INC(nc.vector.tensor_copy(rB[:], psB2[:, 512:1024]), "V")

                    out_done = {}
                    for m2 in range(16):
                        par = m2 % 2
                        if m2 < 2:
                            nc.vector.wait_ge(SEM["STO"], p3_end["STO"])
                        if m2 >= 2:
                            nc.vector.wait_ge(SEM["STO"], out_done[m2 - 2])
                        nc.vector.tensor_tensor(scrB[:, 512:1024], y2pre[:, m2 * 512:(m2 + 1) * 512],
                                                mB[:], SUB)
                        nv = INC(nc.vector.tensor_tensor(bounce[par][:], scrB[:, 512:1024], rB[:], MUL), "V")
                        nc.sync.wait_ge(SEM["V"], nv)
                        out_done[m2] = INC(nc.sync.dma_start(out=out_fm[m2 * 128:(m2 + 1) * 128, hc:hc + 512],
                                                             in_=bounce[par][:]), "STO", 16)
            prev_V = CNT["V"]
            prev_STO = CNT["STO"]

    nc.sync.wait_ge(SEM["STO"], CNT["STO"])
    return nc


_NC_CACHE = {}


def _get_nc():
    if "nc" not in _NC_CACHE:
        _NC_CACHE["nc"] = build_nc()
    return _NC_CACHE["nc"]


def _host_prep(L0_attn_x, L0_attn_wq, L0_attn_wk, L0_attn_wv, L0_attn_wo,
               L0_ffn_w_gate, L0_ffn_w_up, L0_ffn_w_down,
               L0_ffn_s_gate, L0_ffn_s_up, L0_ffn_s_down):
    f32 = np.float32
    x = np.asarray(L0_attn_x, dtype=f32)
    wq = np.asarray(L0_attn_wq, dtype=f32)
    wk = np.asarray(L0_attn_wk, dtype=f32)
    wv = np.asarray(L0_attn_wv, dtype=f32)
    wo = np.asarray(L0_attn_wo, dtype=f32)
    wg = np.asarray(L0_ffn_w_gate)
    wu = np.asarray(L0_ffn_w_up)
    wd = np.asarray(L0_ffn_w_down)

    perm = []
    for g in range(16):
        blk, i_ = g // 4, g % 4
        for hh in (8 * blk + i_, 8 * blk + 4 + i_):
            perm.extend(range(64 * hh, 64 * hh + 64))
    perm = np.array(perm)

    wqT = np.ascontiguousarray(wq[perm].T)
    wqTs = np.ascontiguousarray(wqT.reshape(2048, 4, 512).transpose(1, 0, 2))
    wkTc = np.ascontiguousarray(wk.T)
    wvTc = np.ascontiguousarray(wv.T)
    woT = np.ascontiguousarray(wo.T[perm])
    woSW = np.ascontiguousarray(woT.reshape(16, 128, 16, 128).transpose(2, 1, 0, 3))
    wgSW = np.ascontiguousarray(wg.reshape(16, 128, 64, 128).transpose(2, 1, 0, 3))
    wuSW = np.ascontiguousarray(wu.reshape(16, 128, 64, 128).transpose(2, 1, 0, 3))
    wdSW = np.ascontiguousarray(wd.reshape(64, 128, 16, 128).transpose(2, 1, 0, 3))
    sgT = np.ascontiguousarray(np.asarray(L0_ffn_s_gate, f32).reshape(64, 128).T)
    supT = np.ascontiguousarray(np.asarray(L0_ffn_s_up, f32).reshape(64, 128).T)
    sdT = np.ascontiguousarray(np.asarray(L0_ffn_s_down, f32).reshape(16, 128).T)

    inv = (1.0 / 10000.0 ** (np.arange(0, 64, 2, dtype=f32) / 64.0)).astype(f32)
    t = np.arange(S, dtype=f32)
    fr = np.outer(t, inv).astype(f32)
    cosf = np.cos(fr).astype(f32)
    sinf = np.sin(fr).astype(f32)
    ck8 = np.ascontiguousarray(np.tile(cosf, (1, 8)))
    sk8 = np.ascontiguousarray(np.tile(sinf, (1, 8)))
    cq8f = (ck8 * np.float32(0.125)).astype(f32)
    sq8f = (sk8 * np.float32(0.125)).astype(f32)

    p_ = np.arange(128)[:, None]
    qi = np.arange(512)[None, :]
    maskI = np.stack([np.tile(np.float32(-30000.0) * (128 * s_ + p_ > qi).astype(f32), (1, 2))
                      for s_ in range(4)])

    ident = np.eye(128, dtype=f32)

    xT = [np.ascontiguousarray(x[b].T) for b in range(2)]

    in_maps = []
    for c in range(8):
        b, j = c // 4, c % 4
        acols = slice(512 * j, 512 * j + 512)
        bcols = slice(512 * (7 - j), 512 * (7 - j) + 512)
        xqc = np.ascontiguousarray(np.concatenate([xT[b][:, acols], xT[b][:, bcols]], axis=1))
        cq8 = np.ascontiguousarray(np.concatenate([cq8f[acols.start:acols.stop],
                                                   cq8f[bcols.start:bcols.stop]], axis=0))
        sq8 = np.ascontiguousarray(np.concatenate([sq8f[acols.start:acols.stop],
                                                   sq8f[bcols.start:bcols.stop]], axis=0))
        in_maps.append({
            "xT": xT[b], "xq": xqc,
            "wqTs": wqTs, "wkT": wkTc, "wvT": wvTc, "woSW": woSW,
            "wgSW": wgSW, "wuSW": wuSW, "wdSW": wdSW,
            "sgT": sgT, "supT": supT, "sdT": sdT,
            "ck8": ck8, "sk8": sk8, "cq8": cq8, "sq8": sq8,
            "maskI": maskI, "ident": ident,
        })
    return in_maps


def kernel(**inputs):
    in_maps = _host_prep(**inputs)
    nc = _get_nc()
    res = run_bass_kernel_spmd(nc, in_maps, core_ids=list(range(8)))
    out = np.empty((2, S, D), dtype=np.float32)
    for c in range(8):
        b, j = c // 4, c % 4
        of = np.asarray(res.results[c]["out_fm"])
        out[b, 512 * j:512 * j + 512] = of[:, 0:512].T
        out[b, 512 * (7 - j):512 * (7 - j) + 512] = of[:, 512:1024].T
    return out
